# version 41
# baseline (speedup 1.0000x reference)
import numpy as np
import ml_dtypes
from contextlib import ExitStack

import concourse.bass as bass
import concourse.mybir as mybir
from concourse.bass_utils import run_bass_kernel_spmd

F32 = mybir.dt.float32
BF16 = mybir.dt.bfloat16
U8 = mybir.dt.uint8
AF = mybir.ActivationFunctionType
ALU = mybir.AluOpType
AX = mybir.AxisListType

NCORES = 8
T = 2048
NTB = 16
NCH = 4
D = 1024
H = 16
HD = 64
WS = 8
SHIFT = 4
SEQ = 8192
F0 = 4096
F1 = 4096
NEG = -30000.0
EPS = 1e-5

ENGINES = ("T", "V", "S", "G", "Y")


class Op:
    __slots__ = ("eng", "emit", "deps", "is_dma", "need_inc", "val", "waits", "tag",
                 "semkey", "dma_waits")

    def __init__(self, eng, emit, is_dma, tag="", semkey=None):
        self.eng = eng
        self.emit = emit
        self.deps = []
        self.is_dma = is_dma
        self.need_inc = False
        self.val = None
        self.waits = None
        self.tag = tag
        self.semkey = semkey
        self.dma_waits = {}


class Prog:
    def __init__(self):
        self.streams = {e: [] for e in ENGINES}
        self.order = []
        self.res = {}
        self.overlays = {}
        self.dmacum = {}

    def _dep(self, op, d):
        if d is None or d is op:
            return
        if d.eng == op.eng and not d.is_dma and not op.is_dma:
            if d.eng == "T":
                return
        if d.is_dma:
            if d.is_dma and op.is_dma and d.eng == op.eng and d.semkey == op.semkey:
                return
            k = d.semkey
            v = self.dmacum[k]
            if op.dma_waits.get(k, 0) < v:
                op.dma_waits[k] = v
            return
        op.deps.append(d)
        d.need_inc = True

    def record(self, eng, emit, reads=(), writes=(), deps=(), is_dma=False, tag="",
               semkey=None):
        op = Op(eng, emit, is_dma, tag, semkey)
        for r in reads:
            e = self.res.get(r)
            if e is not None and e[0] is not None:
                self._dep(op, e[0])
        for w in writes:
            e = self.res.get(w)
            if e is not None:
                if e[0] is not None:
                    self._dep(op, e[0])
                for rd in e[1]:
                    self._dep(op, rd)
            elif w in self.overlays:
                for o in self.overlays[w]:
                    oe = self.res.get(o)
                    if oe is not None:
                        if oe[0] is not None:
                            self._dep(op, oe[0])
                        for rd in oe[1]:
                            self._dep(op, rd)
        for d in deps:
            self._dep(op, d)
        if is_dma:
            self.dmacum[semkey] = self.dmacum.get(semkey, 0) + 16
        for r in reads:
            self.res.setdefault(r, [None, []])[1].append(op)
        for w in writes:
            self.res[w] = [op, []]
        self.streams[eng].append(op)
        self.order.append(op)
        return op

    def finalize(self, sems):
        ctr = {e: 0 for e in ENGINES}
        for op in self.order:
            if (not op.is_dma) and op.need_inc:
                ctr[op.eng] += 1
                op.val = ctr[op.eng]
        for e in ENGINES:
            waited = {}
            for op in self.streams[e]:
                ws = []
                for k, v in op.dma_waits.items():
                    if waited.get(k, -1) < v:
                        waited[k] = v
                        ws.append((sems[k], v))
                for d in op.deps:
                    key = "c" + d.eng
                    v = d.val
                    assert v is not None, f"dep without value: {d.tag}"
                    if waited.get(key, -1) < v:
                        waited[key] = v
                        ws.append((sems[key], v))
                op.waits = ws
        self.totals = dict(c=ctr, d=self.dmacum)


def _rel_index():
    coords = np.stack(np.meshgrid(np.arange(WS), np.arange(WS), indexing="ij"))
    cf = coords.reshape(2, -1)
    rel = cf[:, :, None] - cf[:, None, :]
    rel = rel.transpose(1, 2, 0).astype(np.int64)
    rel[..., 0] += WS - 1
    rel[..., 1] += WS - 1
    rel[..., 0] *= 2 * WS - 1
    return rel.sum(-1)[:WS, :WS]


def build_program(n_attn=NCH, n_ffn=NCH, do_conv=True, attn_parts=9):
    nc = bass.Bass("TRN2", target_bir_lowering=False)

    xs = nc.dram_tensor("xs", [T, D], F32, kind="ExternalInput").ap()
    wqkv = nc.dram_tensor("wqkv", [D, 3 * D], F32, kind="ExternalInput").ap()
    wproj = nc.dram_tensor("wproj", [D, D], F32, kind="ExternalInput").ap()
    k0 = nc.dram_tensor("k0", [2 * D, F0], U8, kind="ExternalInput").ap()
    k1 = nc.dram_tensor("k1", [F0, F1], U8, kind="ExternalInput").ap()
    k2 = nc.dram_tensor("k2", [F1, D], U8, kind="ExternalInput").ap()
    ball = nc.dram_tensor("ball", [128, 2048], F32, kind="ExternalInput").ap()
    bqt = nc.dram_tensor("bqt", [128, 24], F32, kind="ExternalInput").ap()
    bpr = nc.dram_tensor("bpr", [128, D], F32, kind="ExternalInput").ap()
    svin = nc.dram_tensor("svin", [128, 3], F32, kind="ExternalInput").ap()
    eyein = nc.dram_tensor("eyein", [128, 128], BF16, kind="ExternalInput").ap()
    onein = nc.dram_tensor("onein", [1, 128], BF16, kind="ExternalInput").ap()
    cstin = nc.dram_tensor("cstin", [128, 2], F32, kind="ExternalInput").ap()
    out = nc.dram_tensor("out", [T, D], F32, kind="ExternalOutput").ap()

    w1scr = nc.dram_tensor("w1scr", [F0, F1], BF16, kind="Internal").ap()
    w0scr = nc.dram_tensor("w0scr", [D, F0], BF16, kind="Internal").ap()
    w2scr = nc.dram_tensor("w2scr", [32, 128, 1024], BF16, kind="Internal").ap()
    wpscr = nc.dram_tensor("wpscr", [8, 128, 1024], BF16, kind="Internal").ap()

    BASE = 16512
    cur = [BASE]

    ranges = {}

    def alloc(name, shape, dt):
        per = int(np.prod(shape[1:])) * mybir.dt.size(dt)
        per = (per + 31) // 32 * 32
        t = nc.alloc_sbuf_tensor_at(name, shape, dt, offset=cur[0])
        ranges[name] = (cur[0], cur[0] + per)
        cur[0] += per
        return t

    def alloc_at(name, shape, dt, off):
        return nc.alloc_sbuf_tensor_at(name, shape, dt, offset=off)

    xbuf = alloc("xbuf", [128, NTB * D], F32)
    ballt = alloc("ballt", [128, 2048], F32)
    bqtt = alloc("bqtt", [128, 24], F32)
    bqvb = alloc("bqvb", [128, 8], BF16)
    svt = alloc("svt", [128, 3], F32)
    sv2 = alloc("sv2", [128, 3], F32)
    eye = alloc("eye", [128, 128], BF16)
    ones1 = alloc("ones1", [1, 128], BF16)
    bprt = alloc("bprt", [128, D], F32)
    beff = alloc("beff", [128, D], F32)
    lnst = alloc("lnst", [128, 160], F32)
    hbuf = alloc("hbuf", [128, D], F32)
    bvwp = alloc_at("bvwp", [1, 1024], BF16, ranges["hbuf"][0])
    cstt = alloc("cstt", [128, 2], F32)

    PH = cur[0]
    cur[0] = PH
    wqkvb = alloc("wqkvb", [128, 8 * 3072], BF16)
    lnx1 = [alloc(f"lnx1_{i}", [128, D], BF16) for i in range(2)]
    lnx1T = alloc("lnx1T", [128, 8 * 512], BF16)
    qkT = alloc("qkT", [128, 16 * 512], BF16)
    vtok = [alloc(f"vtok_{i}", [128, D], BF16) for i in range(3)]
    attnT = alloc("attnT", [128, 8 * 512], BF16)
    sbuf_sm = alloc("sbpln", [128, 2048], F32)
    ubuf = alloc("ubuf", [128, 2048], BF16)
    ranges["sbpln0"] = (ranges["sbpln"][0], ranges["sbpln"][0] + 4096)
    ranges["sbpln1"] = (ranges["sbpln"][0] + 4096, ranges["sbpln"][1])
    ranges["ubuf0"] = (ranges["ubuf"][0], ranges["ubuf"][0] + 2048)
    ranges["ubuf1"] = (ranges["ubuf"][0] + 2048, ranges["ubuf"][1])
    utb = alloc("utb", [128, 2048], BF16)
    ranges["utb0"] = (ranges["utb"][0], ranges["utb"][0] + 2048)
    ranges["utb1"] = (ranges["utb"][0] + 2048, ranges["utb"][1])
    wp = [alloc(f"wp_{i}", [128, D], BF16) for i in range(2)]
    cvb = alloc("cvb", [128, 4096], BF16)
    cv = [cvb[:, 0:2048], cvb[:, 2048:4096]]
    ranges["cv_0"] = (ranges["cvb"][0], ranges["cvb"][0] + 4096)
    ranges["cv_1"] = (ranges["cvb"][0] + 4096, ranges["cvb"][1])
    cvu8 = [alloc(f"cu_{i}", [128, 2048], U8) for i in range(2)]
    attn_end = cur[0]

    cur[0] = PH
    w1b = [alloc(f"w1b_{i}", [128, 32 * 256], BF16) for i in range(2)]
    w0b = [alloc(f"w0b_{i}", [128, 2048], BF16) for i in range(2)]
    w2b = [alloc(f"w2b_{i}", [128, 2048], BF16) for i in range(2)]
    ln3T = [alloc("ln3T_0", [128, 8 * 512], BF16)]
    _l3 = alloc("lnx3_0", [128, D], BF16)
    lnx3 = [_l3, _l3]
    g0T = alloc("g0T", [128, 32 * 512], BF16)
    g1T = alloc("g1T", [128, 32 * 512], BF16)
    ffn_end = cur[0]
    w0A = alloc_at("w0A", [128, 4096], BF16, ranges["w0b_0"][0])
    ranges["w0A"] = (ranges["w0b_0"][0], ranges["w0b_1"][1])
    w0B = alloc_at("w0B", [128, 4096], BF16, ranges["ballt"][0])
    ranges["w0B"] = ranges["ballt"]
    w2A = alloc_at("w2A", [128, 4096], BF16, ranges["w2b_0"][0])
    ranges["w2A"] = (ranges["w2b_0"][0], ranges["w2b_1"][1])
    w2B = alloc_at("w2B", [128, 4096], BF16, ranges["bprt"][0])
    ranges["w2B"] = (ranges["bprt"][0], ranges["beff"][1])
    wpA = alloc_at("wpA", [128, 2048], BF16, ranges["wp_0"][0])
    ranges["wpA"] = (ranges["wp_0"][0], ranges["wp_1"][1])
    wpB = alloc_at("wpB", [128, 2048], BF16, ranges["bprt"][0])
    ranges["wpB"] = ranges["bprt"]

    LIMIT = 229376 - 192
    assert attn_end <= LIMIT, f"attn region overflow {attn_end}"
    assert ffn_end <= LIMIT, f"ffn region overflow {ffn_end}"

    ATTN_NAMES = ["wqkvb", "lnx1_0", "lnx1_1", "lnx1T", "qkT", "vtok_0", "vtok_1",
                  "vtok_2", "attnT", "sbpln0", "sbpln1", "ubuf0", "ubuf1",
                  "utb0", "utb1", "wp_0", "wp_1",
                  "cv_0", "cv_1", "cu_0", "cu_1"]
    FFN_NAMES = ["w1b_0", "w1b_1", "w0A", "w2A", "ln3T_0",
                 "lnx3_0", "g0T", "g1T"]

    def _overlaps(a, b):
        (a0, a1), (b0, b1) = ranges[a], ranges[b]
        return a0 < b1 and b0 < a1

    P = Prog()
    for _fb in FFN_NAMES:
        P.overlays[_fb] = [a for a in ATTN_NAMES if _overlaps(_fb, a)]
    P.overlays["hbuf"] = ["bvwp"]
    P.overlays["w0B"] = ["ballt"]
    P.overlays["w2B"] = ["bprt", "beff", "wpB"]
    P.overlays["wpB"] = ["bprt"]

    st = ExitStack()
    pS = st.enter_context(nc.psum_tensor("pS", [128, 2048], F32))
    p2 = st.enter_context(nc.psum_tensor("p2", [128, 1024], F32))
    pt = st.enter_context(nc.psum_tensor("pt", [128, 2048], BF16))
    SEMKEYS = ("cT", "cV", "cS", "cG", "cY", "setup", "wqkvc", "cv0", "cv1", "cu0", "cu1",
               "sw0", "sw1", "wp0", "wp1", "w0b0", "w0b1",
               "w1b0", "w1b1", "w2b0", "w2b1", "outq")
    sem_handles = {}
    for name in SEMKEYS:
        sem_handles[name] = st.enter_context(nc.semaphore(name))

    def T_(emit, reads=(), writes=(), deps=(), tag=""):
        return P.record("T", emit, reads, writes, deps, tag=tag)

    def V_(emit, reads=(), writes=(), deps=(), tag=""):
        return P.record("V", emit, reads, writes, deps, tag=tag)

    def S_(emit, reads=(), writes=(), deps=(), tag=""):
        return P.record("S", emit, reads, writes, deps, tag=tag)

    def G_(emit, reads=(), writes=(), deps=(), tag=""):
        return P.record("G", emit, reads, writes, deps, tag=tag)

    def DMA(eng, semkey, outap, inap, reads=(), writes=(), deps=(), tag=""):
        return P.record(eng, (lambda e, o=outap, i=inap: e.dma_start(o, i)),
                        reads, writes, deps, is_dma=True, tag=tag, semkey=semkey)

    ld = {}
    for tb in range(NTB):
        ld[f"x{tb}"] = DMA("Y", "setup", xbuf[:, tb * D:(tb + 1) * D], xs[tb * 128:(tb + 1) * 128, :],
                           writes=[f"x{tb}"], tag=f"ldx{tb}")
    DMA("Y", "setup", ballt[:], ball, writes=["ballt"])
    DMA("Y", "setup", bqtt[:], bqt, writes=["bqtt"])
    DMA("Y", "setup", svt[:], svin, writes=["svt"])
    DMA("Y", "setup", eye[:], eyein, writes=["eye"])
    DMA("Y", "setup", ones1[:], onein, writes=["ones1"])
    DMA("Y", "setup", bprt[:], bpr, writes=["bprt"])
    DMA("Y", "setup", cstt[:], cstin, writes=["cstt"])

    for c in range(8):
        DMA("G", "wqkvc", wqkvb[:, c * 3072:(c + 1) * 3072], wqkv[c * 128:(c + 1) * 128, :],
            writes=["wqkvb"], tag=f"wqkvcast{c}")

    G_(lambda e: e.tensor_scalar(sv2[:, 0:1], svt[:, 0:1], 4.0, None, ALU.mult),
       reads=["svt"], writes=["sv2"])
    G_(lambda e: e.tensor_scalar(sv2[:, 1:2], svt[:, 1:2], 2.0, None, ALU.mult),
       reads=["svt"], writes=["sv2"])
    G_(lambda e: e.tensor_copy(sv2[:, 2:3], svt[:, 2:3]), reads=["svt"], writes=["sv2"])
    G_(lambda e: e.tensor_copy(bqvb[:], bqtt[:, 16:24]), reads=["bqtt"], writes=["bqvb"])

    def mk_conv_jobs():
        jobs = []
        for c in range(8):
            jobs.append(("wp", c, 0))
        for c in range(32):
            for h in range(2):
                jobs.append(("w1", c, h))
        for c in range(8):
            for q in range(4):
                jobs.append(("w0", c, q))
        for c in range(32):
            jobs.append(("w2", c, 0))
        return jobs

    def _convkey(j):
        return j[0]

    conv_jobs = mk_conv_jobs()
    conv_last = {}
    for _i, _j in enumerate(conv_jobs):
        conv_last[_convkey(_j)] = _i
    markers = {}

    def conv_cast(i):
        kind, a, b2 = conv_jobs[i]
        b = i % 2
        if kind == "wp":
            DMA("G", f"cv{b}", cv[b][:, 0:1024], wproj[a * 128:(a + 1) * 128, :],
                writes=[f"cv_{b}"], tag=f"wpcast{a}")
        elif kind == "w2":
            DMA("Y", f"cu{b}", cvu8[b][:, 0:1024], k2[a * 128:(a + 1) * 128, :],
                writes=[f"cu_{b}"], tag=f"w2c{a}")
        elif kind == "w0":
            DMA("Y", f"cu{b}", cvu8[b][:].rearrange("p (h j) -> p h j", h=2),
                bass.AP(k0.tensor, a * 128 * F0 + b2 * 1024,
                        [[F0, 128], [D * F0, 2], [1, 1024]]),
                writes=[f"cu_{b}"], tag=f"w0c{a}_{b2}")
        else:
            DMA("Y", f"cu{b}", cvu8[b][:],
                k1[a * 128:(a + 1) * 128, b2 * 2048:(b2 + 1) * 2048],
                writes=[f"cu_{b}"], tag=f"w1c{a}_{b2}")

    def conv_work(i):
        kind, a, b2 = conv_jobs[i]
        b = i % 2
        if kind == "wp":
            DMA("G", f"sw{b}", wpscr[a], cv[b][:, 0:1024], reads=[f"cv_{b}"],
                writes=[f"wpscr_w{b}"], tag=f"wpw{a}")
        elif kind == "w2":
            G_(lambda e, b=b: e.tensor_scalar(cv[b][:, 0:1024], cvu8[b][:, 0:1024],
                                              2.0, -1.0, ALU.mult, ALU.add),
               reads=[f"cu_{b}"], writes=[f"cv_{b}"], tag=f"w2aff{a}")
            if a % 2 == 1:
                DMA("G", "sw0",
                    bass.AP(w2scr.tensor, (a - 1) * 128 * 1024,
                            [[1024, 128], [128 * 1024, 2], [1, 1024]]),
                    cvb[:].rearrange("p (b j) -> p b j", b=2)[:, :, 0:1024],
                    reads=["cv_0", "cv_1"], writes=["w2scr_w0"], tag=f"w2w{a}")
        elif kind == "w0":
            G_(lambda e, b=b: e.tensor_tensor(cv[b][:, 0:1024], cvu8[b][:, 0:1024],
                                              cvu8[b][:, 1024:2048], ALU.subtract),
               reads=[f"cu_{b}"], writes=[f"cv_{b}"], tag=f"w0sub{a}_{b2}")
            if b2 % 2 == 1:
                DMA("G", "sw0",
                    w0scr[a * 128:(a + 1) * 128, (b2 - 1) * 1024:(b2 + 1) * 1024]
                    .rearrange("p (b j) -> p b j", b=2),
                    cvb[:].rearrange("p (b j) -> p b j", b=2)[:, :, 0:1024],
                    reads=["cv_0", "cv_1"], writes=["w0scr_w0"], tag=f"w0w{a}_{b2}")
        else:
            G_(lambda e, b=b: e.tensor_scalar(cv[b][:], cvu8[b][:], 2.0, -1.0,
                                              ALU.mult, ALU.add),
               reads=[f"cu_{b}"], writes=[f"cv_{b}"], tag=f"w1aff{a}_{b2}")
            if b2 == 1:
                DMA("G", "sw0", w1scr[a * 128:(a + 1) * 128, :], cvb[:],
                    reads=["cv_0", "cv_1"], writes=["w1scr_w0"], tag=f"w1w{a}")
        key = _convkey(conv_jobs[i])
        if conv_last[key] == i:
            rd = {"wp": ["wpscr_w0", "wpscr_w1"], "w1": ["w1scr_w0"],
                  "w0": ["w0scr_w0"], "w2": ["w2scr_w0"]}[key]
            markers[key] = G_(lambda e: e.engine_nop(), reads=rd, tag=f"{key}mark")

    conv_pos = [0, 0]

    def run_conv(n):
        if not do_conv:
            return
        k = 0
        while k < n and conv_pos[0] < len(conv_jobs):
            while conv_pos[1] <= conv_pos[0] + 1 and conv_pos[1] < len(conv_jobs):
                conv_cast(conv_pos[1])
                conv_pos[1] += 1
            conv_work(conv_pos[0])
            conv_pos[0] += 1
            k += 1

    def ln_chain(src_ap_fn, slot, res_src, tag, scratch=None, scres=None):
        c0 = slot * 16
        stat = lnst
        sres = f"lnst{slot}"
        V_(lambda e: e.bn_stats(stat[:, c0 + 4:c0 + 10], src_ap_fn(0)),
           reads=[res_src], writes=[sres], tag=tag + "bn0")
        V_(lambda e: e.bn_stats(stat[:, c0 + 10:c0 + 16], src_ap_fn(1)),
           reads=[res_src], writes=[sres], tag=tag + "bn1")
        V_(lambda e: e.bn_aggr(stat[:, c0:c0 + 2], stat[:, c0 + 4:c0 + 16]),
           reads=[], writes=[sres], tag=tag + "ag")
        S_(lambda e: e.activation(stat[:, c0 + 2:c0 + 3], stat[:, c0 + 1:c0 + 2], AF.Ln,
                                  bias=cstt[:, 0:1], scale=1.0),
           reads=["cstt"], writes=[sres], tag=tag + "ln")
        S_(lambda e: e.activation(stat[:, c0 + 3:c0 + 4], stat[:, c0 + 2:c0 + 3], AF.Exp,
                                  bias=0.0, scale=-0.5),
           reads=[], writes=[sres], tag=tag + "ex")
        V_(lambda e: e.tensor_scalar(stat[:, c0 + 4:c0 + 5], stat[:, c0:c0 + 1],
                                     stat[:, c0 + 3:c0 + 4], -1.0, ALU.mult, ALU.mult),
           reads=[], writes=[sres], tag=tag + "b")
        return stat[:, c0 + 3:c0 + 4], stat[:, c0 + 4:c0 + 5], sres

    def ln_stats4(ch, tagp):
        out = {}
        for i in range(4):
            tb = 4 * ch + i
            c0 = (4 + i) * 16
            for half in range(2):
                V_(lambda e, tb=tb, c0=c0, half=half:
                   e.bn_stats(lnst[:, c0 + 4 + 6 * half: c0 + 10 + 6 * half],
                              xbuf[:, tb * D + half * 512: tb * D + (half + 1) * 512]),
                   reads=[f"x{tb}"], writes=[f"lnst{4 + i}"], tag=f"{tagp}bn{tb}_{half}")
            V_(lambda e, c0=c0: e.bn_aggr(lnst[:, c0:c0 + 2], lnst[:, c0 + 4:c0 + 16]),
               reads=[], writes=[f"lnst{4 + i}"], tag=f"{tagp}ag{tb}")
        wr = [f"lnst{4 + i}" for i in range(4)]
        S_(lambda e: e.activation(lnst[:, 66:115:16], lnst[:, 65:114:16], AF.Ln,
                                  bias=cstt[:, 0:1], scale=1.0),
           reads=["cstt"], writes=wr, tag=f"{tagp}ln4x{ch}")
        S_(lambda e: e.activation(lnst[:, 67:116:16], lnst[:, 66:115:16], AF.Exp,
                                  bias=0.0, scale=-0.5),
           reads=[], writes=wr, tag=f"{tagp}ex4x{ch}")
        V_(lambda e: e.scalar_tensor_tensor(lnst[:, 68:117:16], lnst[:, 64:113:16], -1.0,
                                            lnst[:, 67:116:16], ALU.mult, ALU.mult),
           reads=[], writes=wr, tag=f"{tagp}b4x{ch}")
        for i in range(4):
            c0 = (4 + i) * 16
            out[4 * ch + i] = (lnst[:, c0 + 3:c0 + 4], lnst[:, c0 + 4:c0 + 5],
                               f"lnst{4 + i}")
        return out

    ln3_sb = {}

    def ln3_tr(ch, i):
        tb = 4 * ch + i
        par = tb % 2
        lpar = 0
        rstd, bv, sres = ln3_sb[tb]
        V_(lambda e, tb=tb, par=par, rstd=rstd, bv=bv:
           e.tensor_scalar(lnx3[par][:], xbuf[:, tb * D:(tb + 1) * D], rstd, bv,
                           ALU.mult, ALU.add),
           reads=[f"x{tb}", sres], writes=["lnx3_0"], tag=f"ln3ap{tb}")
        ptres = f"pt{par}"
        for f in range(8):
            T_(lambda e, par=par, f=f:
               e.transpose(pt[:, par * 1024 + f * 128: par * 1024 + (f + 1) * 128],
                           lnx3[par][:, f * 128:(f + 1) * 128], eye[:]),
               reads=["lnx3_0", "eye"], writes=[ptres], tag=f"tr3_{tb}_{f}")
        V_(lambda e, par=par, i=i, lpar=lpar:
           e.tensor_copy(
               ln3T[lpar][:].rearrange("p (c t) -> p c t", c=8)[:, :, i * 128:(i + 1) * 128],
               pt[:, par * 1024:(par + 1) * 1024].rearrange("p (c t) -> p c t", c=8)),
           reads=[ptres], writes=[f"ln3T_{lpar}"], tag=f"evtr3_{tb}")

    def ln3_block(ch):
        ln3_sb.update(ln_stats4(ch, "ln3_"))
        for i in range(4):
            ln3_tr(ch, i)

    ln3_done = []

    ln1_sb = {}

    def ln1_tr(ch, i, dst, dres):
        tb = 4 * ch + i
        par = tb % 2
        rstd, bv, sres = ln1_sb[tb]
        V_(lambda e, tb=tb, par=par, rstd=rstd, bv=bv:
           e.tensor_scalar(lnx1[par][:], xbuf[:, tb * D:(tb + 1) * D], rstd, bv,
                           ALU.mult, ALU.add),
           reads=[f"x{tb}", sres], writes=[f"lnx1_{par}"], tag=f"ln1ap{tb}")
        for f in range(8):
            T_(lambda e, dst=dst, par=par, f=f:
               e.transpose(dst[:, f * 128:(f + 1) * 128],
                           lnx1[par][:, f * 128:(f + 1) * 128], eye[:]),
               reads=[f"lnx1_{par}", "eye"], writes=dres, tag=f"tr1_{tb}_{f}")
        S_(lambda e, dst=dst, i=i:
           e.copy(
               lnx1T[:].rearrange("p (c t) -> p c t", c=8)[:, :, i * 128:(i + 1) * 128],
               dst.rearrange("p (c t) -> p c t", c=8)),
           reads=dres, writes=["lnx1T"], tag=f"evtr1_{tb}")

    def ln1_block(ch):
        ln1_sb.update(ln_stats4(ch, "ln1_"))
        for i in range(4):
            par = (4 * ch + i) % 2
            ln1_tr(ch, i, pt[:, par * 1024:(par + 1) * 1024], [f"pt{par}"])

    ln1_block(0)
    for ch in range(n_attn):
        tbs = [4 * ch + i for i in range(4)]
        run_conv(9)

        for fop in (range(8) if attn_parts >= 2 else []):
            slot = fop % 2
            for c in range(8):
                for f2 in range(2):
                    fo = fop * 2 + f2
                    T_(lambda e, slot=slot, f2=f2, c=c, fo=fo:
                       e.matmul(pS[:, (slot * 2 + f2) * 512:(slot * 2 + f2 + 1) * 512],
                                wqkvb[:, c * 3072 + fo * 128: c * 3072 + (fo + 1) * 128],
                                lnx1T[:, c * 512:(c + 1) * 512],
                                start=(c == 0), stop=(c == 7)),
                       reads=["wqkvb", "lnx1T"], writes=[f"pS{slot * 2 + f2}"],
                       tag=f"qk{ch}_{fo}_{c}")
            for f2 in range(2):
                fo = fop * 2 + f2
                sl = slot * 2 + f2
                S_(lambda e, sl=sl, fo=fo:
                   e.add(qkT[:, fo * 512:(fo + 1) * 512], pS[:, sl * 512:(sl + 1) * 512],
                         bqtt[:, fo:fo + 1]),
                   reads=[f"pS{sl}", "bqtt"], writes=["qkT"], tag=f"qkev{ch}_{fo}")
        if ch + 1 < n_attn:
            ln1_sb.update(ln_stats4(ch + 1, "ln1_"))
        run_conv(9)

        def v_mm(i):
            par = i % 3
            for c in range(8):
                for half in range(2):
                    T_(lambda e, c=c, half=half, i=i:
                       e.matmul(p2[:, half * 512:(half + 1) * 512],
                                lnx1T[:, c * 512 + i * 128: c * 512 + (i + 1) * 128],
                                wqkvb[:, c * 3072 + 2048 + half * 512:
                                      c * 3072 + 2048 + (half + 1) * 512],
                                start=(c == 0), stop=(c == 7)),
                       reads=["lnx1T", "wqkvb"], writes=["p2"], tag=f"v{ch}_{i}_{c}_{half}")
            S_(lambda e, par=par: e.activation(vtok[par][:], p2[:], AF.Copy, bias=0.0,
                                               scale=1.0),
               reads=["p2"], writes=[f"vtok_{par}"], tag=f"vev{ch}_{i}")

        def s_mm(i):
            for h in [0, 2, 4, 6, 8, 10, 12, 14, 1, 3, 5, 7, 9, 11, 13, 15]:
                bp = 64 * (h % 2)
                fo = h // 2
                T_(lambda e, h=h, bp=bp, fo=fo, i=i:
                   e.matmul(pS[:, h * 128:(h + 1) * 128],
                            qkT[bp:bp + 64, fo * 512 + i * 128: fo * 512 + (i + 1) * 128],
                            qkT[bp:bp + 64, (8 + fo) * 512 + i * 128: (8 + fo) * 512 + (i + 1) * 128],
                            start=True, stop=True),
                   reads=["qkT"], writes=[f"pS{h // 4}"], tag=f"s{ch}_{i}_{h}")

        def softmax(i):
            for hh in range(2):
                V_(lambda e, hh=hh: e.scalar_tensor_tensor(
                       sbuf_sm[:, hh * 1024:(hh + 1) * 1024],
                       pS[:, hh * 1024:(hh + 1) * 1024], 0.125,
                       ballt[:, hh * 1024:(hh + 1) * 1024], ALU.mult, ALU.add),
                   reads=[f"pS{2 * hh}", f"pS{2 * hh + 1}", "ballt"],
                   writes=[f"sbpln{hh}"], tag=f"sb{ch}_{i}_{hh}")
            for hh in range(2):
                S_(lambda e, hh=hh: e.activation(
                       ubuf[:, hh * 1024:(hh + 1) * 1024],
                       sbuf_sm[:, hh * 1024:(hh + 1) * 1024],
                       AF.Exp, bias=cstt[:, 1:2], scale=1.0),
                   reads=[f"sbpln{hh}", "cstt"], writes=[f"ubuf{hh}"],
                   tag=f"exp{ch}_{i}_{hh}")
            for hh in range(2):
                V_(lambda e, hh=hh: e.reduce_sum(
                       lnst[:, 128 + 8 * hh:136 + 8 * hh],
                       ubuf[:, hh * 1024:(hh + 1) * 1024]
                       .rearrange("p (h j) -> p h j", h=8), axis=AX.X),
                   reads=[f"ubuf{hh}"], writes=[f"lnr{hh}"], tag=f"rsum{ch}_{i}_{hh}")
                V_(lambda e, hh=hh: e.reciprocal(lnst[:, 144 + 8 * hh:152 + 8 * hh],
                                                 lnst[:, 128 + 8 * hh:136 + 8 * hh]),
                   reads=[f"lnr{hh}"], writes=[f"lnrec{hh}"], tag=f"rec{ch}_{i}_{hh}")
                V_(lambda e, hh=hh: e.tensor_tensor(
                       ubuf[:, hh * 1024:(hh + 1) * 1024]
                       .rearrange("p (h j) -> p h j", h=8),
                       ubuf[:, hh * 1024:(hh + 1) * 1024]
                       .rearrange("p (h j) -> p h j", h=8),
                       lnst[:, 144 + 8 * hh:152 + 8 * hh].unsqueeze(2)
                       .broadcast_to([128, 8, 128]), ALU.mult),
                   reads=[f"lnrec{hh}"], writes=[f"ubuf{hh}"], tag=f"norm{ch}_{i}_{hh}")

        def ut_av(i):
            par = i % 3
            for hh in range(2):
                ptres = f"pt{hh}"
                for hl in range(8):
                    h = hh * 8 + hl
                    T_(lambda e, hh=hh, hl=hl, h=h:
                       e.transpose(pt[:, hh * 1024 + hl * 128: hh * 1024 + (hl + 1) * 128],
                                   ubuf[:, h * 128:(h + 1) * 128], eye[:]),
                       reads=[f"ubuf{hh}", "eye"], writes=[ptres], tag=f"trU{ch}_{i}_{h}")
                S_(lambda e, hh=hh:
                   e.copy(utb[:, hh * 1024:(hh + 1) * 1024], pt[:, hh * 1024:(hh + 1) * 1024]),
                   reads=[ptres], writes=[f"utb{hh}"], tag=f"utev{ch}_{i}_{hh}")
                for h in [hh * 8 + o for o in (0, 2, 4, 6, 1, 3, 5, 7)]:
                    T_(lambda e, h=h, par=par, hh=hh:
                       e.matmul(p2[64 * (h % 2):64 * (h % 2) + 64,
                                   (h // 2) * 128:(h // 2 + 1) * 128],
                                vtok[par][:, h * 64:(h + 1) * 64],
                                utb[:, h * 128:(h + 1) * 128], start=True, stop=True),
                       reads=[f"vtok_{par}", f"utb{hh}"], writes=["p2"],
                       tag=f"av{ch}_{i}_{h}")
            V_(lambda e, i=i:
               e.tensor_copy(
                   attnT[:].rearrange("p (g t) -> p g t", g=8)[:, :, i * 128:(i + 1) * 128],
                   p2[:].rearrange("p (g t) -> p g t", g=8)),
               reads=["p2"], writes=["attnT"], tag=f"avev{ch}_{i}")

        if attn_parts < 4:
            run_conv(9)
            continue
        if attn_parts < 8:
            for _i in range(4):
                v_mm(_i)
                if attn_parts >= 5:
                    s_mm(_i)
                if attn_parts >= 6:
                    softmax(_i)
                if attn_parts >= 7:
                    ut_av(_i)
            run_conv(9)
            continue
        v_mm(0)
        s_mm(0)
        softmax(0)
        v_mm(1)
        s_mm(1)
        ut_av(0)
        softmax(1)
        v_mm(2)
        s_mm(2)
        ut_av(1)
        softmax(2)
        v_mm(3)
        s_mm(3)
        ut_av(2)
        softmax(3)
        ut_av(3)
        run_conv(9)

        if ch == 0 and attn_parts >= 3:
            V_(lambda e: e.tensor_copy(beff[:], bprt[:]),
               reads=["bprt"], writes=["beff"], tag="beffcp")
            for c in range(8):
                cp = c // 2
                buf, bres = (wpA, "wpA") if cp % 2 == 0 else (wpB, "wpB")
                if c % 2 == 0:
                    DMA("S", f"wp{cp % 2}", buf[:],
                        bass.AP(wpscr.tensor, 2 * cp * 128 * 1024,
                                [[1024, 128], [128 * 1024, 2], [1, 1024]]),
                        deps=[markers["wp"]], writes=[bres], tag=f"wpld_beff{cp}")
                for half in range(2):
                    T_(lambda e, buf=buf, bres=bres, c=c, half=half:
                       e.matmul(p2[0:1, half * 512:(half + 1) * 512], bqvb[:, c:c + 1],
                                buf[:, (c % 2) * 1024 + half * 512:
                                    (c % 2) * 1024 + (half + 1) * 512],
                                start=(c == 0), stop=(c == 7)),
                       reads=[bres, "bqvb"], writes=["p2"], tag=f"beffmm{c}_{half}")
            S_(lambda e: e.copy(bvwp[:], p2[0:1, :]), reads=["p2"], writes=["bvwp"],
               tag="beffev")
            for half in range(2):
                T_(lambda e, half=half:
                   e.matmul(p2[:, half * 512:(half + 1) * 512], ones1[:],
                            bvwp[0:1, half * 512:(half + 1) * 512], start=True, stop=True),
                   reads=["bvwp", "ones1"], writes=["p2"], tag=f"beffbc{half}")
            V_(lambda e: e.tensor_add(beff[:], p2[:], beff[:]),
               reads=["p2", "beff"], writes=["beff"], tag="beffadd")


        ptf = pt[:].bitcast(F32)
        pjdsts = [pS[:, 0:1024], pS[:, 1024:2048], p2[:], ptf]
        pjres = [["pS0", "pS1"], ["pS2", "pS3"], ["p2"], ["pt0", "pt1"]]
        if attn_parts >= 9:
            for c in range(8):
                cp = c // 2
                buf, bres = (wpA, "wpA") if cp % 2 == 0 else (wpB, "wpB")
                if c % 2 == 0:
                    DMA("S", f"wp{cp % 2}", buf[:],
                        bass.AP(wpscr.tensor, 2 * cp * 128 * 1024,
                                [[1024, 128], [128 * 1024, 2], [1, 1024]]),
                        deps=[markers["wp"]], writes=[bres], tag=f"wpld{ch}_{cp}")
                for i in range(4):
                    for half in range(2):
                        T_(lambda e, buf=buf, bres=bres, c=c, i=i, half=half:
                           e.matmul(pjdsts[i][:, half * 512:(half + 1) * 512],
                                    attnT[:, c * 512 + i * 128: c * 512 + (i + 1) * 128],
                                    buf[:, (c % 2) * 1024 + half * 512:
                                        (c % 2) * 1024 + (half + 1) * 512],
                                    start=(c == 0), stop=(c == 7)),
                           reads=["attnT", bres],
                           writes=pjres[i], tag=f"pj{ch}_{i}_{c}_{half}")
            pSb = pS[:].bitcast(BF16)
            atf = attnT[:].bitcast(F32)
            pstage = [sbuf_sm[:, 0:1024], sbuf_sm[:, 1024:2048],
                      atf[:, 0:1024], atf[:, 1024:2048]]
            psres = ["sbpln0", "sbpln1", "attnT", "attnT"]
            for i in range(4):
                V_(lambda e, i=i:
                   e.scalar_tensor_tensor(pstage[i], pjdsts[i], 1.0,
                                          beff[:], ALU.mult, ALU.add),
                   reads=pjres[i] + ["beff"], writes=[psres[i]],
                   tag=f"pjev{ch}_{i}")
                if ch + 1 < n_attn:
                    ln1_tr(ch + 1, i, pSb[:, i * 1024:(i + 1) * 1024], [f"pS{i}"])
            if ch + 1 == n_attn and n_ffn:
                ln3_block(0)
                ln3_done.append(1)
            for i in range(4):
                tb = tbs[i]
                rstd, bv, sres = ln_chain(
                    lambda hf, i=i: pstage[i][:, hf * 512:(hf + 1) * 512], i,
                    psres[i], f"ln2_{tb}",
                    scratch=ubuf[:].bitcast(F32)[:, 512:1024], scres="ubuf")
                V_(lambda e, i=i, rstd=rstd, bv=bv:
                   e.tensor_scalar(pstage[i], pstage[i], rstd, bv, ALU.mult, ALU.add),
                   reads=[sres], writes=[psres[i]], tag=f"ln2ap{tb}")
                V_(lambda e, tb=tb, i=i:
                   e.tensor_add(xbuf[:, tb * D:(tb + 1) * D], xbuf[:, tb * D:(tb + 1) * D],
                                pstage[i]),
                   reads=[psres[i]], writes=[f"x{tb}"], tag=f"res2_{tb}")
        run_conv(9)

    if n_ffn and not ln3_done:
        ln3_block(0)
    for ch in range(n_ffn):
        tbs = [4 * ch + i for i in range(4)]
        lpar = 0

        def w0_load(j, ch=ch):
            buf, bres = (w0A, "w0A") if j % 2 == 0 else (w0B, "w0B")
            qeng = "S" if (ch == 0 and j < 4) else "Y"
            DMA(qeng, f"w0b{j % 2}", buf[:].rearrange("p (c j) -> p c j", c=8),
                bass.AP(w0scr.tensor, j * 512,
                        [[F0, 128], [128 * F0, 8], [1, 512]]),
                deps=[markers["w0"]], writes=[bres], tag=f"w0ld{ch}_{j}")

        for j in range(8):
            buf, bres = (w0A, "w0A") if j % 2 == 0 else (w0B, "w0B")
            if j >= 2 or ch == 0:
                w0_load(j)
            for m in range(4):
                f0b = 4 * j + m
                slot = f0b % 4
                for c in range(8):
                    T_(lambda e, buf=buf, bres=bres, m=m, c=c, slot=slot, lpar=lpar:
                       e.matmul(pS[:, slot * 512:(slot + 1) * 512],
                                buf[:, c * 512 + m * 128: c * 512 + m * 128 + 128],
                                ln3T[lpar][:, c * 512:(c + 1) * 512],
                                start=(c == 0), stop=(c == 7)),
                       reads=[bres, f"ln3T_{lpar}"], writes=[f"pS{slot}"],
                       tag=f"l0_{ch}_{f0b}_{c}")
                if m % 2 == 1:
                    fp = f0b // 2
                    pr = fp % 2
                    S_(lambda e, fp=fp, pr=pr:
                       e.activation(g0T[:, 2 * fp * 512:(2 * fp + 2) * 512],
                                    pS[:, pr * 1024:(pr + 1) * 1024], AF.Relu,
                                    bias=0.0, scale=sv2[:, 0:1]),
                       reads=[f"pS{pr * 2}", f"pS{pr * 2 + 1}", "sv2"], writes=["g0T"],
                       tag=f"g0ev{ch}_{fp}")
                    if fp == 0 and ch + 1 < n_ffn:
                        ln3_sb.update(ln_stats4(ch + 1, "ln3_"))

        def w2_load(j, ch=ch):
            buf, bres = (w2A, "w2A") if j % 2 == 0 else (w2B, "w2B")
            DMA("Y", f"w2b{j % 2}", buf[:].rearrange("p (b j) -> p b j", b=4),
                bass.AP(w2scr.tensor, 4 * j * 128 * 1024,
                        [[1024, 128], [128 * 1024, 4], [1, 1024]]),
                deps=[markers["w2"]], writes=[bres], tag=f"w2ld{ch}_{j}")

        w2_load(0)
        w2_load(1)
        if ch + 1 < n_ffn:
            for _i in range(4):
                ln3_tr(ch + 1, _i)
        if ch + 1 < n_ffn:
            w0_load(0, ch + 1)
            w0_load(1, ch + 1)

        for s in range(16):
            wb = s % 2
            DMA("Y", f"w1b{wb}",
                w1b[wb][:].rearrange("p (c j) -> p c j", c=32),
                bass.AP(w1scr.tensor, s * 256,
                        [[F1, 128], [128 * F1, 32], [1, 256]]),
                deps=[markers["w1"]],
                writes=[f"w1b_{wb}q{g}" for g in range(4)], tag=f"w1ld{ch}_{s}")
            base = 2 * (s % 2)
            for c in range(32):
                for fl in range(2):
                    T_(lambda e, wb=wb, c=c, fl=fl, base=base:
                       e.matmul(pS[:, (base + fl) * 512:(base + fl + 1) * 512],
                                w1b[wb][:, c * 256 + fl * 128: c * 256 + fl * 128 + 128],
                                g0T[:, c * 512:(c + 1) * 512],
                                start=(c == 0), stop=(c == 31)),
                       reads=[f"w1b_{wb}q{c // 8}", "g0T"], writes=[f"pS{base + fl}"],
                       tag=f"w1_{ch}_{s}_{c}_{fl}")
            S_(lambda e, s=s, base=base:
               e.activation(g1T[:, s * 1024:(s + 1) * 1024],
                            pS[:, (base // 2) * 1024:(base // 2 + 1) * 1024], AF.Relu,
                            bias=0.0, scale=sv2[:, 1:2]),
               reads=[f"pS{base}", f"pS{base + 1}", "sv2"], writes=["g1T"],
               tag=f"g1ev{ch}_{s}")


        ptf = pt[:].bitcast(F32)
        w2dsts = [pS[:, 0:1024], pS[:, 1024:2048], p2[:], ptf]
        w2res = [["pS0", "pS1"], ["pS2", "pS3"], ["p2"], ["pt0", "pt1"]]
        for fcp in range(16):
            j = fcp // 2
            buf, bres = (w2A, "w2A") if j % 2 == 0 else (w2B, "w2B")
            q = fcp % 2
            if fcp >= 4 and fcp % 2 == 0:
                w2_load(fcp // 2)
            for fl in range(2):
                fc = 2 * fcp + fl
                for i in range(4):
                    for half in range(2):
                        T_(lambda e, buf=buf, bres=bres, q=q, fl=fl, fc=fc, half=half, i=i:
                           e.matmul(w2dsts[i][:, half * 512:(half + 1) * 512],
                                    g1T[:, fc * 512 + i * 128: fc * 512 + (i + 1) * 128],
                                    buf[:, q * 2048 + fl * 1024 + half * 512:
                                        q * 2048 + fl * 1024 + (half + 1) * 512],
                                    start=(fc == 0), stop=(fc == 31)),
                           reads=[bres, "g1T"], writes=w2res[i],
                           tag=f"w2_{ch}_{i}_{fc}_{half}")
        hbufs = [(hbuf[:], "hbuf"), (w2A[:].bitcast(F32)[:, 0:1024], "w2A")]

        def h2ev(i, ch=ch):
            hb, hres = hbufs[i % 2]
            V_(lambda e, i=i, hb=hb:
               e.tensor_scalar(hb, w2dsts[i], sv2[:, 2:3], None, ALU.mult),
               reads=w2res[i] + ["sv2"], writes=[hres], tag=f"h2ev{ch}_{i}")

        def ln4_fin(i):
            tb = tbs[i]
            hb, hres = hbufs[i % 2]
            rstd, bv, sres = ln_chain(
                lambda half, hb=hb: hb[:, half * 512:(half + 1) * 512], i, hres,
                f"ln4_{tb}",
                scratch=lnx3[0][:].bitcast(F32)[:, 0:512], scres="lnx3_0")
            V_(lambda e, hb=hb, rstd=rstd, bv=bv:
               e.tensor_scalar(hb, hb, rstd, bv, ALU.mult, ALU.add),
               reads=[sres], writes=[hres], tag=f"ln4ap{tb}")
            V_(lambda e, tb=tb, hb=hb:
               e.tensor_add(xbuf[:, tb * D:(tb + 1) * D], xbuf[:, tb * D:(tb + 1) * D],
                            hb),
               reads=[hres], writes=[f"x{tb}"], tag=f"res4_{tb}")
            DMA("S", "outq", out[tb * 128:(tb + 1) * 128, :], xbuf[:, tb * D:(tb + 1) * D],
                reads=[f"x{tb}"], tag=f"st{tb}")

        h2ev(0)
        h2ev(1)
        ln4_fin(0)
        h2ev(2)
        ln4_fin(1)
        h2ev(3)
        ln4_fin(2)
        ln4_fin(3)

    assert (not do_conv) or n_attn < NCH or conv_pos[0] == len(conv_jobs), \
        f"conv jobs left: {len(conv_jobs) - conv_pos[0]}"
    if do_conv and n_attn == NCH:
        assert set(markers) == {"wp", "w0", "w1", "w2"}, markers.keys()

    P.finalize(sem_handles)

    def make_body(ename):
        stream = P.streams[ename]
        csem = sem_handles["c" + ename]

        def body(eng):
            for op in stream:
                for (sem, val) in op.waits:
                    eng.wait_ge(sem, val)
                if op.emit is None:
                    continue
                inst = op.emit(eng)
                if op.is_dma:
                    inst.then_inc(sem_handles[op.semkey], 16)
                elif op.need_inc:
                    inst.then_inc(csem, 1)
        return body

    fin = Op("Y", None, False, "fin")
    fin.waits = []
    for k, v in P.totals["d"].items():
        fin.waits.append((sem_handles[k], v))
    for e in ENGINES:
        if P.totals["c"][e]:
            fin.waits.append((sem_handles["c" + e], P.totals["c"][e]))
    P.streams["Y"].append(fin)

    with nc.Block() as block:
        block.tensor(make_body("T"))
        block.vector(make_body("V"))
        block.scalar(make_body("S"))
        block.gpsimd(make_body("G"))
        block.sync(make_body("Y"))
    st.close()
    return nc


_NC_CACHE = []


def kernel(L0_attn_x, L0_attn_w_qkv, L0_attn_b_qkv, L0_attn_w_proj, L0_attn_b_proj,
           L0_attn_bias_table, L0_ffn_k0, L0_ffn_k1, L0_ffn_k2, L0_ffn_s0, L0_ffn_s1,
           L0_ffn_s2):
    x = np.ascontiguousarray(np.asarray(L0_attn_x, np.float32))
    wqkv = np.ascontiguousarray(np.asarray(L0_attn_w_qkv, np.float32))
    wproj = np.ascontiguousarray(np.asarray(L0_attn_w_proj, np.float32))
    bqkv = np.asarray(L0_attn_b_qkv, np.float32)
    bproj = np.asarray(L0_attn_b_proj, np.float32)
    btab = np.asarray(L0_attn_bias_table, np.float32)
    k0 = np.ascontiguousarray(np.asarray(L0_ffn_k0).astype(np.uint8))
    k1 = np.ascontiguousarray(np.asarray(L0_ffn_k1).astype(np.uint8))
    k2 = np.ascontiguousarray(np.asarray(L0_ffn_k2).astype(np.uint8))
    s0 = np.float32(L0_ffn_s0)
    s1 = np.float32(L0_ffn_s1)
    s2 = np.float32(L0_ffn_s2)

    idx = _rel_index()
    bias = btab[idx.reshape(-1), :].reshape(WS, WS, H)
    ball = np.full((128, 2048), NEG, np.float32)
    for h in range(H):
        for a in range(16):
            ball[a * 8:(a + 1) * 8, h * 128 + a * 8: h * 128 + (a + 1) * 8] = bias[:, :, h]

    bqt = np.ascontiguousarray(bqkv.reshape(24, 128).T)
    bpr = np.ascontiguousarray(np.broadcast_to(bproj[None, :], (128, 1024)))
    sv = np.ascontiguousarray(np.broadcast_to(
        np.stack([s0, s1, s2])[None, :], (128, 3))).astype(np.float32)
    eyeb = np.eye(128, dtype=np.float32).astype(ml_dtypes.bfloat16)
    oneb = np.ones((1, 128), ml_dtypes.bfloat16)
    cst = np.ascontiguousarray(np.broadcast_to(
        np.array([EPS, -20.0], np.float32)[None, :], (128, 2)))

    rows = []
    in_maps = []
    for core in range(NCORES):
        b, r = divmod(core, 4)
        rr = (np.arange(2048 * r, 2048 * (r + 1)) + SHIFT) % SEQ
        rows.append((b, rr))
        xs = np.ascontiguousarray(x[b, rr, :])
        in_maps.append(dict(xs=xs, wqkv=wqkv, wproj=wproj, k0=k0, k1=k1, k2=k2,
                            ball=ball, bqt=bqt, bpr=bpr, svin=sv, eyein=eyeb,
                            onein=oneb, cstin=cst))

    if not _NC_CACHE:
        _NC_CACHE.append(build_program())
    nc = _NC_CACHE[0]
    res = run_bass_kernel_spmd(nc, in_maps, core_ids=list(range(NCORES)))

    outp = np.empty((2, SEQ, D), np.float32)
    for core in range(NCORES):
        b, rr = rows[core]
        outp[b, rr, :] = np.asarray(res.results[core]["out"], np.float32)
    return outp


# version 42
# speedup vs baseline: 1.0145x; 1.0145x over previous
import numpy as np
import ml_dtypes
from contextlib import ExitStack

import concourse.bass as bass
import concourse.mybir as mybir
from concourse.bass_utils import run_bass_kernel_spmd

F32 = mybir.dt.float32
BF16 = mybir.dt.bfloat16
U8 = mybir.dt.uint8
AF = mybir.ActivationFunctionType
ALU = mybir.AluOpType
AX = mybir.AxisListType

NCORES = 8
T = 2048
NTB = 16
NCH = 4
D = 1024
H = 16
HD = 64
WS = 8
SHIFT = 4
SEQ = 8192
F0 = 4096
F1 = 4096
NEG = -30000.0
EPS = 1e-5

ENGINES = ("T", "V", "S", "G", "Y")


class Op:
    __slots__ = ("eng", "emit", "deps", "is_dma", "need_inc", "val", "waits", "tag",
                 "semkey", "dma_waits")

    def __init__(self, eng, emit, is_dma, tag="", semkey=None):
        self.eng = eng
        self.emit = emit
        self.deps = []
        self.is_dma = is_dma
        self.need_inc = False
        self.val = None
        self.waits = None
        self.tag = tag
        self.semkey = semkey
        self.dma_waits = {}


class Prog:
    def __init__(self):
        self.streams = {e: [] for e in ENGINES}
        self.order = []
        self.res = {}
        self.overlays = {}
        self.dmacum = {}

    def _dep(self, op, d):
        if d is None or d is op:
            return
        if d.eng == op.eng and not d.is_dma and not op.is_dma:
            if d.eng == "T":
                return
        if d.is_dma:
            if d.is_dma and op.is_dma and d.eng == op.eng and d.semkey == op.semkey:
                return
            k = d.semkey
            v = self.dmacum[k]
            if op.dma_waits.get(k, 0) < v:
                op.dma_waits[k] = v
            return
        op.deps.append(d)
        d.need_inc = True

    def record(self, eng, emit, reads=(), writes=(), deps=(), is_dma=False, tag="",
               semkey=None):
        op = Op(eng, emit, is_dma, tag, semkey)
        for r in reads:
            e = self.res.get(r)
            if e is not None and e[0] is not None:
                self._dep(op, e[0])
        for w in writes:
            e = self.res.get(w)
            if e is not None:
                if e[0] is not None:
                    self._dep(op, e[0])
                for rd in e[1]:
                    self._dep(op, rd)
            elif w in self.overlays:
                for o in self.overlays[w]:
                    oe = self.res.get(o)
                    if oe is not None:
                        if oe[0] is not None:
                            self._dep(op, oe[0])
                        for rd in oe[1]:
                            self._dep(op, rd)
        for d in deps:
            self._dep(op, d)
        if is_dma:
            self.dmacum[semkey] = self.dmacum.get(semkey, 0) + 16
        for r in reads:
            self.res.setdefault(r, [None, []])[1].append(op)
        for w in writes:
            self.res[w] = [op, []]
        self.streams[eng].append(op)
        self.order.append(op)
        return op

    def finalize(self, sems):
        ctr = {e: 0 for e in ENGINES}
        for op in self.order:
            if (not op.is_dma) and op.need_inc:
                ctr[op.eng] += 1
                op.val = ctr[op.eng]
        for e in ENGINES:
            waited = {}
            for op in self.streams[e]:
                ws = []
                for k, v in op.dma_waits.items():
                    if waited.get(k, -1) < v:
                        waited[k] = v
                        ws.append((sems[k], v))
                for d in op.deps:
                    key = "c" + d.eng
                    v = d.val
                    assert v is not None, f"dep without value: {d.tag}"
                    if waited.get(key, -1) < v:
                        waited[key] = v
                        ws.append((sems[key], v))
                op.waits = ws
        self.totals = dict(c=ctr, d=self.dmacum)


def _rel_index():
    coords = np.stack(np.meshgrid(np.arange(WS), np.arange(WS), indexing="ij"))
    cf = coords.reshape(2, -1)
    rel = cf[:, :, None] - cf[:, None, :]
    rel = rel.transpose(1, 2, 0).astype(np.int64)
    rel[..., 0] += WS - 1
    rel[..., 1] += WS - 1
    rel[..., 0] *= 2 * WS - 1
    return rel.sum(-1)[:WS, :WS]


def build_program(n_attn=NCH, n_ffn=NCH, do_conv=True, attn_parts=9):
    nc = bass.Bass("TRN2", target_bir_lowering=False)

    xs = nc.dram_tensor("xs", [T, D], F32, kind="ExternalInput").ap()
    wqkv = nc.dram_tensor("wqkv", [D, 3 * D], F32, kind="ExternalInput").ap()
    wproj = nc.dram_tensor("wproj", [D, D], F32, kind="ExternalInput").ap()
    k0 = nc.dram_tensor("k0", [2 * D, F0], U8, kind="ExternalInput").ap()
    k1 = nc.dram_tensor("k1", [F0, F1], U8, kind="ExternalInput").ap()
    k2 = nc.dram_tensor("k2", [F1, D], U8, kind="ExternalInput").ap()
    ball = nc.dram_tensor("ball", [128, 2048], F32, kind="ExternalInput").ap()
    bqt = nc.dram_tensor("bqt", [128, 24], F32, kind="ExternalInput").ap()
    bpr = nc.dram_tensor("bpr", [128, D], F32, kind="ExternalInput").ap()
    svin = nc.dram_tensor("svin", [128, 3], F32, kind="ExternalInput").ap()
    eyein = nc.dram_tensor("eyein", [128, 128], BF16, kind="ExternalInput").ap()
    onein = nc.dram_tensor("onein", [1, 128], BF16, kind="ExternalInput").ap()
    cstin = nc.dram_tensor("cstin", [128, 2], F32, kind="ExternalInput").ap()
    out = nc.dram_tensor("out", [T, D], F32, kind="ExternalOutput").ap()

    w1scr = nc.dram_tensor("w1scr", [F0, F1], BF16, kind="Internal").ap()
    w0scr = nc.dram_tensor("w0scr", [D, F0], BF16, kind="Internal").ap()
    w2scr = nc.dram_tensor("w2scr", [32, 128, 1024], BF16, kind="Internal").ap()
    wpscr = nc.dram_tensor("wpscr", [8, 128, 1024], BF16, kind="Internal").ap()

    BASE = 16512
    cur = [BASE]

    ranges = {}

    def alloc(name, shape, dt):
        per = int(np.prod(shape[1:])) * mybir.dt.size(dt)
        per = (per + 31) // 32 * 32
        t = nc.alloc_sbuf_tensor_at(name, shape, dt, offset=cur[0])
        ranges[name] = (cur[0], cur[0] + per)
        cur[0] += per
        return t

    def alloc_at(name, shape, dt, off):
        return nc.alloc_sbuf_tensor_at(name, shape, dt, offset=off)

    xbuf = alloc("xbuf", [128, NTB * D], F32)
    ballt = alloc("ballt", [128, 2048], F32)
    bqtt = alloc("bqtt", [128, 24], F32)
    bqvb = alloc("bqvb", [128, 8], BF16)
    svt = alloc("svt", [128, 3], F32)
    sv2 = alloc("sv2", [128, 3], F32)
    eye = alloc("eye", [128, 128], BF16)
    ones1 = alloc("ones1", [1, 128], BF16)
    bprt = alloc("bprt", [128, D], F32)
    beff = alloc("beff", [128, D], F32)
    lnst = alloc("lnst", [128, 160], F32)
    hbuf = alloc("hbuf", [128, D], F32)
    bvwp = alloc_at("bvwp", [1, 1024], BF16, ranges["hbuf"][0])
    cstt = alloc("cstt", [128, 2], F32)

    PH = cur[0]
    cur[0] = PH
    wqkvb = alloc("wqkvb", [128, 8 * 3072], BF16)
    lnx1 = [alloc(f"lnx1_{i}", [128, D], BF16) for i in range(2)]
    lnx1T = alloc("lnx1T", [128, 8 * 512], BF16)
    qkT = alloc("qkT", [128, 16 * 512], BF16)
    vtok = [alloc(f"vtok_{i}", [128, D], BF16) for i in range(3)]
    attnT = alloc("attnT", [128, 8 * 512], BF16)
    sbuf_sm = alloc("sbpln", [128, 2048], F32)
    ubuf = alloc("ubuf", [128, 2048], BF16)
    ranges["sbpln0"] = (ranges["sbpln"][0], ranges["sbpln"][0] + 4096)
    ranges["sbpln1"] = (ranges["sbpln"][0] + 4096, ranges["sbpln"][1])
    ranges["ubuf0"] = (ranges["ubuf"][0], ranges["ubuf"][0] + 2048)
    ranges["ubuf1"] = (ranges["ubuf"][0] + 2048, ranges["ubuf"][1])
    utb = alloc("utb", [128, 2048], BF16)
    ranges["utb0"] = (ranges["utb"][0], ranges["utb"][0] + 2048)
    ranges["utb1"] = (ranges["utb"][0] + 2048, ranges["utb"][1])
    wp = [alloc(f"wp_{i}", [128, D], BF16) for i in range(2)]
    cvb = alloc("cvb", [128, 4096], BF16)
    cv = [cvb[:, 0:2048], cvb[:, 2048:4096]]
    ranges["cv_0"] = (ranges["cvb"][0], ranges["cvb"][0] + 4096)
    ranges["cv_1"] = (ranges["cvb"][0] + 4096, ranges["cvb"][1])
    cvu8 = [alloc(f"cu_{i}", [128, 2048], U8) for i in range(2)]
    attn_end = cur[0]

    cur[0] = PH
    w1b = [alloc(f"w1b_{i}", [128, 32 * 256], BF16) for i in range(2)]
    w0b = [alloc(f"w0b_{i}", [128, 2048], BF16) for i in range(2)]
    w2b = [alloc(f"w2b_{i}", [128, 2048], BF16) for i in range(2)]
    ln3T = [alloc("ln3T_0", [128, 8 * 512], BF16)]
    _l3 = alloc("lnx3_0", [128, D], BF16)
    lnx3 = [_l3, _l3]
    g0T = alloc("g0T", [128, 32 * 512], BF16)
    g1T = alloc("g1T", [128, 32 * 512], BF16)
    ffn_end = cur[0]
    w0A = alloc_at("w0A", [128, 4096], BF16, ranges["w0b_0"][0])
    ranges["w0A"] = (ranges["w0b_0"][0], ranges["w0b_1"][1])
    w0B = alloc_at("w0B", [128, 4096], BF16, ranges["ballt"][0])
    ranges["w0B"] = ranges["ballt"]
    w2A = alloc_at("w2A", [128, 4096], BF16, ranges["w2b_0"][0])
    ranges["w2A"] = (ranges["w2b_0"][0], ranges["w2b_1"][1])
    w2B = alloc_at("w2B", [128, 4096], BF16, ranges["bprt"][0])
    ranges["w2B"] = (ranges["bprt"][0], ranges["beff"][1])
    wpA = alloc_at("wpA", [128, 2048], BF16, ranges["wp_0"][0])
    ranges["wpA"] = (ranges["wp_0"][0], ranges["wp_1"][1])
    wpB = alloc_at("wpB", [128, 2048], BF16, ranges["bprt"][0])
    ranges["wpB"] = ranges["bprt"]

    LIMIT = 229376 - 192
    assert attn_end <= LIMIT, f"attn region overflow {attn_end}"
    assert ffn_end <= LIMIT, f"ffn region overflow {ffn_end}"

    ATTN_NAMES = ["wqkvb", "lnx1_0", "lnx1_1", "lnx1T", "qkT", "vtok_0", "vtok_1",
                  "vtok_2", "attnT", "sbpln0", "sbpln1", "ubuf0", "ubuf1",
                  "utb0", "utb1", "wp_0", "wp_1",
                  "cv_0", "cv_1", "cu_0", "cu_1"]
    FFN_NAMES = ["w1b_0", "w1b_1", "w0A", "w2A", "ln3T_0",
                 "lnx3_0", "g0T", "g1T"]

    def _overlaps(a, b):
        (a0, a1), (b0, b1) = ranges[a], ranges[b]
        return a0 < b1 and b0 < a1

    P = Prog()
    for _fb in FFN_NAMES:
        P.overlays[_fb] = [a for a in ATTN_NAMES if _overlaps(_fb, a)]
    P.overlays["hbuf"] = ["bvwp"]
    P.overlays["w0B"] = ["ballt"]
    P.overlays["w2B"] = ["bprt", "beff", "wpB"]
    P.overlays["wpB"] = ["bprt"]

    st = ExitStack()
    pS = st.enter_context(nc.psum_tensor("pS", [128, 2048], F32))
    p2 = st.enter_context(nc.psum_tensor("p2", [128, 1024], F32))
    pt = st.enter_context(nc.psum_tensor("pt", [128, 2048], BF16))
    SEMKEYS = ("cT", "cV", "cS", "cG", "cY", "setup", "wqkvc", "cv0", "cv1", "cu0", "cu1",
               "sw0", "sw1", "wp0", "wp1", "w0b0", "w0b1",
               "w1b0", "w1b1", "w2b0", "w2b1", "outq")
    sem_handles = {}
    for name in SEMKEYS:
        sem_handles[name] = st.enter_context(nc.semaphore(name))

    def T_(emit, reads=(), writes=(), deps=(), tag=""):
        return P.record("T", emit, reads, writes, deps, tag=tag)

    def V_(emit, reads=(), writes=(), deps=(), tag=""):
        return P.record("V", emit, reads, writes, deps, tag=tag)

    def S_(emit, reads=(), writes=(), deps=(), tag=""):
        return P.record("S", emit, reads, writes, deps, tag=tag)

    def G_(emit, reads=(), writes=(), deps=(), tag=""):
        return P.record("G", emit, reads, writes, deps, tag=tag)

    def DMA(eng, semkey, outap, inap, reads=(), writes=(), deps=(), tag=""):
        return P.record(eng, (lambda e, o=outap, i=inap: e.dma_start(o, i)),
                        reads, writes, deps, is_dma=True, tag=tag, semkey=semkey)

    ld = {}
    for tb in range(NTB):
        ld[f"x{tb}"] = DMA("Y", "setup", xbuf[:, tb * D:(tb + 1) * D], xs[tb * 128:(tb + 1) * 128, :],
                           writes=[f"x{tb}"], tag=f"ldx{tb}")
    DMA("Y", "setup", ballt[:], ball, writes=["ballt"])
    DMA("Y", "setup", bqtt[:], bqt, writes=["bqtt"])
    DMA("Y", "setup", svt[:], svin, writes=["svt"])
    DMA("Y", "setup", eye[:], eyein, writes=["eye"])
    DMA("Y", "setup", ones1[:], onein, writes=["ones1"])
    DMA("Y", "setup", bprt[:], bpr, writes=["bprt"])
    DMA("Y", "setup", cstt[:], cstin, writes=["cstt"])

    for c in range(8):
        DMA("G", "wqkvc", wqkvb[:, c * 3072:(c + 1) * 3072], wqkv[c * 128:(c + 1) * 128, :],
            writes=["wqkvb"], tag=f"wqkvcast{c}")

    G_(lambda e: e.tensor_scalar(sv2[:, 0:1], svt[:, 0:1], 4.0, None, ALU.mult),
       reads=["svt"], writes=["sv2"])
    G_(lambda e: e.tensor_scalar(sv2[:, 1:2], svt[:, 1:2], 2.0, None, ALU.mult),
       reads=["svt"], writes=["sv2"])
    G_(lambda e: e.tensor_copy(sv2[:, 2:3], svt[:, 2:3]), reads=["svt"], writes=["sv2"])
    G_(lambda e: e.tensor_copy(bqvb[:], bqtt[:, 16:24]), reads=["bqtt"], writes=["bqvb"])

    def mk_conv_jobs():
        jobs = []
        for c in range(8):
            jobs.append(("wp", c, 0))
        for c in range(32):
            for h in range(2):
                jobs.append(("w1", c, h))
        for c in range(8):
            for q in range(4):
                jobs.append(("w0", c, q))
        for c in range(32):
            jobs.append(("w2", c, 0))
        return jobs

    def _convkey(j):
        return j[0]

    conv_jobs = mk_conv_jobs()
    conv_last = {}
    for _i, _j in enumerate(conv_jobs):
        conv_last[_convkey(_j)] = _i
    markers = {}

    def conv_cast(i):
        kind, a, b2 = conv_jobs[i]
        b = i % 2
        if kind == "wp":
            DMA("G", f"cv{b}", cv[b][:, 0:1024], wproj[a * 128:(a + 1) * 128, :],
                writes=[f"cv_{b}"], tag=f"wpcast{a}")
        elif kind == "w2":
            DMA("Y", f"cu{b}", cvu8[b][:, 0:1024], k2[a * 128:(a + 1) * 128, :],
                writes=[f"cu_{b}"], tag=f"w2c{a}")
        elif kind == "w0":
            DMA("Y", f"cu{b}", cvu8[b][:].rearrange("p (h j) -> p h j", h=2),
                bass.AP(k0.tensor, a * 128 * F0 + b2 * 1024,
                        [[F0, 128], [D * F0, 2], [1, 1024]]),
                writes=[f"cu_{b}"], tag=f"w0c{a}_{b2}")
        else:
            DMA("Y", f"cu{b}", cvu8[b][:],
                k1[a * 128:(a + 1) * 128, b2 * 2048:(b2 + 1) * 2048],
                writes=[f"cu_{b}"], tag=f"w1c{a}_{b2}")

    def conv_work(i):
        kind, a, b2 = conv_jobs[i]
        b = i % 2
        if kind == "wp":
            DMA("G", f"sw{b}", wpscr[a], cv[b][:, 0:1024], reads=[f"cv_{b}"],
                writes=[f"wpscr_w{b}"], tag=f"wpw{a}")
        elif kind == "w2":
            G_(lambda e, b=b: e.tensor_scalar(cv[b][:, 0:1024], cvu8[b][:, 0:1024],
                                              2.0, -1.0, ALU.mult, ALU.add),
               reads=[f"cu_{b}"], writes=[f"cv_{b}"], tag=f"w2aff{a}")
            if a % 2 == 1:
                DMA("G", "sw0",
                    bass.AP(w2scr.tensor, (a - 1) * 128 * 1024,
                            [[1024, 128], [128 * 1024, 2], [1, 1024]]),
                    cvb[:].rearrange("p (b j) -> p b j", b=2)[:, :, 0:1024],
                    reads=["cv_0", "cv_1"], writes=["w2scr_w0"], tag=f"w2w{a}")
        elif kind == "w0":
            G_(lambda e, b=b: e.tensor_tensor(cv[b][:, 0:1024], cvu8[b][:, 0:1024],
                                              cvu8[b][:, 1024:2048], ALU.subtract),
               reads=[f"cu_{b}"], writes=[f"cv_{b}"], tag=f"w0sub{a}_{b2}")
            if b2 % 2 == 1:
                DMA("G", "sw0",
                    w0scr[a * 128:(a + 1) * 128, (b2 - 1) * 1024:(b2 + 1) * 1024]
                    .rearrange("p (b j) -> p b j", b=2),
                    cvb[:].rearrange("p (b j) -> p b j", b=2)[:, :, 0:1024],
                    reads=["cv_0", "cv_1"], writes=["w0scr_w0"], tag=f"w0w{a}_{b2}")
        else:
            G_(lambda e, b=b: e.tensor_scalar(cv[b][:], cvu8[b][:], 2.0, -1.0,
                                              ALU.mult, ALU.add),
               reads=[f"cu_{b}"], writes=[f"cv_{b}"], tag=f"w1aff{a}_{b2}")
            if b2 == 1:
                DMA("G", "sw0", w1scr[a * 128:(a + 1) * 128, :], cvb[:],
                    reads=["cv_0", "cv_1"], writes=["w1scr_w0"], tag=f"w1w{a}")
        key = _convkey(conv_jobs[i])
        if conv_last[key] == i:
            rd = {"wp": ["wpscr_w0", "wpscr_w1"], "w1": ["w1scr_w0"],
                  "w0": ["w0scr_w0"], "w2": ["w2scr_w0"]}[key]
            markers[key] = G_(lambda e: e.engine_nop(), reads=rd, tag=f"{key}mark")

    conv_pos = [0, 0]

    def run_conv(n):
        if not do_conv:
            return
        k = 0
        while k < n and conv_pos[0] < len(conv_jobs):
            while conv_pos[1] <= conv_pos[0] + 1 and conv_pos[1] < len(conv_jobs):
                conv_cast(conv_pos[1])
                conv_pos[1] += 1
            conv_work(conv_pos[0])
            conv_pos[0] += 1
            k += 1

    def ln_chain(src_ap_fn, slot, res_src, tag, scratch=None, scres=None):
        c0 = slot * 16
        stat = lnst
        sres = f"lnst{slot}"
        V_(lambda e: e.bn_stats(stat[:, c0 + 4:c0 + 10], src_ap_fn(0)),
           reads=[res_src], writes=[sres], tag=tag + "bn0")
        V_(lambda e: e.bn_stats(stat[:, c0 + 10:c0 + 16], src_ap_fn(1)),
           reads=[res_src], writes=[sres], tag=tag + "bn1")
        V_(lambda e: e.bn_aggr(stat[:, c0:c0 + 2], stat[:, c0 + 4:c0 + 16]),
           reads=[], writes=[sres], tag=tag + "ag")
        S_(lambda e: e.activation(stat[:, c0 + 2:c0 + 3], stat[:, c0 + 1:c0 + 2], AF.Ln,
                                  bias=cstt[:, 0:1], scale=1.0),
           reads=["cstt"], writes=[sres], tag=tag + "ln")
        S_(lambda e: e.activation(stat[:, c0 + 3:c0 + 4], stat[:, c0 + 2:c0 + 3], AF.Exp,
                                  bias=0.0, scale=-0.5),
           reads=[], writes=[sres], tag=tag + "ex")
        V_(lambda e: e.tensor_scalar(stat[:, c0 + 4:c0 + 5], stat[:, c0:c0 + 1],
                                     stat[:, c0 + 3:c0 + 4], -1.0, ALU.mult, ALU.mult),
           reads=[], writes=[sres], tag=tag + "b")
        return stat[:, c0 + 3:c0 + 4], stat[:, c0 + 4:c0 + 5], sres

    def ln_stats4(ch, tagp):
        out = {}
        for i in range(4):
            tb = 4 * ch + i
            c0 = (4 + i) * 16
            for half in range(2):
                V_(lambda e, tb=tb, c0=c0, half=half:
                   e.bn_stats(lnst[:, c0 + 4 + 6 * half: c0 + 10 + 6 * half],
                              xbuf[:, tb * D + half * 512: tb * D + (half + 1) * 512]),
                   reads=[f"x{tb}"], writes=[f"lnst{4 + i}"], tag=f"{tagp}bn{tb}_{half}")
            V_(lambda e, c0=c0: e.bn_aggr(lnst[:, c0:c0 + 2], lnst[:, c0 + 4:c0 + 16]),
               reads=[], writes=[f"lnst{4 + i}"], tag=f"{tagp}ag{tb}")
        wr = [f"lnst{4 + i}" for i in range(4)]
        S_(lambda e: e.activation(lnst[:, 66:115:16], lnst[:, 65:114:16], AF.Ln,
                                  bias=cstt[:, 0:1], scale=1.0),
           reads=["cstt"], writes=wr, tag=f"{tagp}ln4x{ch}")
        S_(lambda e: e.activation(lnst[:, 67:116:16], lnst[:, 66:115:16], AF.Exp,
                                  bias=0.0, scale=-0.5),
           reads=[], writes=wr, tag=f"{tagp}ex4x{ch}")
        V_(lambda e: e.scalar_tensor_tensor(lnst[:, 68:117:16], lnst[:, 64:113:16], -1.0,
                                            lnst[:, 67:116:16], ALU.mult, ALU.mult),
           reads=[], writes=wr, tag=f"{tagp}b4x{ch}")
        for i in range(4):
            c0 = (4 + i) * 16
            out[4 * ch + i] = (lnst[:, c0 + 3:c0 + 4], lnst[:, c0 + 4:c0 + 5],
                               f"lnst{4 + i}")
        return out

    ln3_sb = {}

    def ln3_tr(ch, i):
        tb = 4 * ch + i
        par = tb % 2
        lpar = 0
        rstd, bv, sres = ln3_sb[tb]
        V_(lambda e, tb=tb, par=par, rstd=rstd, bv=bv:
           e.tensor_scalar(lnx3[par][:], xbuf[:, tb * D:(tb + 1) * D], rstd, bv,
                           ALU.mult, ALU.add),
           reads=[f"x{tb}", sres], writes=["lnx3_0"], tag=f"ln3ap{tb}")
        ptres = f"pt{par}"
        for f in range(8):
            T_(lambda e, par=par, f=f:
               e.transpose(pt[:, par * 1024 + f * 128: par * 1024 + (f + 1) * 128],
                           lnx3[par][:, f * 128:(f + 1) * 128], eye[:]),
               reads=["lnx3_0", "eye"], writes=[ptres], tag=f"tr3_{tb}_{f}")
        V_(lambda e, par=par, i=i, lpar=lpar:
           e.tensor_copy(
               ln3T[lpar][:].rearrange("p (c t) -> p c t", c=8)[:, :, i * 128:(i + 1) * 128],
               pt[:, par * 1024:(par + 1) * 1024].rearrange("p (c t) -> p c t", c=8)),
           reads=[ptres], writes=[f"ln3T_{lpar}"], tag=f"evtr3_{tb}")

    def ln3_block(ch):
        ln3_sb.update(ln_stats4(ch, "ln3_"))
        for i in range(4):
            ln3_tr(ch, i)

    ln3_done = []

    ln1_sb = {}

    def ln1_tr(ch, i, dst, dres):
        tb = 4 * ch + i
        par = tb % 2
        rstd, bv, sres = ln1_sb[tb]
        V_(lambda e, tb=tb, par=par, rstd=rstd, bv=bv:
           e.tensor_scalar(lnx1[par][:], xbuf[:, tb * D:(tb + 1) * D], rstd, bv,
                           ALU.mult, ALU.add),
           reads=[f"x{tb}", sres], writes=[f"lnx1_{par}"], tag=f"ln1ap{tb}")
        for f in range(8):
            T_(lambda e, dst=dst, par=par, f=f:
               e.transpose(dst[:, f * 128:(f + 1) * 128],
                           lnx1[par][:, f * 128:(f + 1) * 128], eye[:]),
               reads=[f"lnx1_{par}", "eye"], writes=dres, tag=f"tr1_{tb}_{f}")
        S_(lambda e, dst=dst, i=i:
           e.copy(
               lnx1T[:].rearrange("p (c t) -> p c t", c=8)[:, :, i * 128:(i + 1) * 128],
               dst.rearrange("p (c t) -> p c t", c=8)),
           reads=dres, writes=["lnx1T"], tag=f"evtr1_{tb}")

    def ln1_block(ch):
        ln1_sb.update(ln_stats4(ch, "ln1_"))
        for i in range(4):
            par = (4 * ch + i) % 2
            ln1_tr(ch, i, pt[:, par * 1024:(par + 1) * 1024], [f"pt{par}"])

    ln1_block(0)
    for ch in range(n_attn):
        tbs = [4 * ch + i for i in range(4)]
        run_conv(9)

        for fop in (range(8) if attn_parts >= 2 else []):
            slot = fop % 2
            for c in range(8):
                for f2 in range(2):
                    fo = fop * 2 + f2
                    T_(lambda e, slot=slot, f2=f2, c=c, fo=fo:
                       e.matmul(pS[:, (slot * 2 + f2) * 512:(slot * 2 + f2 + 1) * 512],
                                wqkvb[:, c * 3072 + fo * 128: c * 3072 + (fo + 1) * 128],
                                lnx1T[:, c * 512:(c + 1) * 512],
                                start=(c == 0), stop=(c == 7)),
                       reads=["wqkvb", "lnx1T"], writes=[f"pS{slot * 2 + f2}"],
                       tag=f"qk{ch}_{fo}_{c}")
            for f2 in range(2):
                fo = fop * 2 + f2
                sl = slot * 2 + f2
                S_(lambda e, sl=sl, fo=fo:
                   e.add(qkT[:, fo * 512:(fo + 1) * 512], pS[:, sl * 512:(sl + 1) * 512],
                         bqtt[:, fo:fo + 1]),
                   reads=[f"pS{sl}", "bqtt"], writes=["qkT"], tag=f"qkev{ch}_{fo}")
        if ch + 1 < n_attn:
            ln1_sb.update(ln_stats4(ch + 1, "ln1_"))
        run_conv(9)

        def v_mm(i):
            par = i % 3
            for c in range(8):
                for half in range(2):
                    T_(lambda e, c=c, half=half, i=i:
                       e.matmul(p2[:, half * 512:(half + 1) * 512],
                                lnx1T[:, c * 512 + i * 128: c * 512 + (i + 1) * 128],
                                wqkvb[:, c * 3072 + 2048 + half * 512:
                                      c * 3072 + 2048 + (half + 1) * 512],
                                start=(c == 0), stop=(c == 7)),
                       reads=["lnx1T", "wqkvb"], writes=["p2"], tag=f"v{ch}_{i}_{c}_{half}")
            S_(lambda e, par=par: e.activation(vtok[par][:], p2[:], AF.Copy, bias=0.0,
                                               scale=1.0),
               reads=["p2"], writes=[f"vtok_{par}"], tag=f"vev{ch}_{i}")

        def s_mm(i):
            for h in [0, 2, 4, 6, 8, 10, 12, 14, 1, 3, 5, 7, 9, 11, 13, 15]:
                bp = 64 * (h % 2)
                fo = h // 2
                T_(lambda e, h=h, bp=bp, fo=fo, i=i:
                   e.matmul(pS[:, h * 128:(h + 1) * 128],
                            qkT[bp:bp + 64, fo * 512 + i * 128: fo * 512 + (i + 1) * 128],
                            qkT[bp:bp + 64, (8 + fo) * 512 + i * 128: (8 + fo) * 512 + (i + 1) * 128],
                            start=True, stop=True),
                   reads=["qkT"], writes=[f"pS{h // 4}"], tag=f"s{ch}_{i}_{h}")

        def softmax(i):
            for hh in range(2):
                V_(lambda e, hh=hh: e.scalar_tensor_tensor(
                       sbuf_sm[:, hh * 1024:(hh + 1) * 1024],
                       pS[:, hh * 1024:(hh + 1) * 1024], 0.125,
                       ballt[:, hh * 1024:(hh + 1) * 1024], ALU.mult, ALU.add),
                   reads=[f"pS{2 * hh}", f"pS{2 * hh + 1}", "ballt"],
                   writes=[f"sbpln{hh}"], tag=f"sb{ch}_{i}_{hh}")
            for hh in range(2):
                S_(lambda e, hh=hh: e.activation(
                       ubuf[:, hh * 1024:(hh + 1) * 1024],
                       sbuf_sm[:, hh * 1024:(hh + 1) * 1024],
                       AF.Exp, bias=cstt[:, 1:2], scale=1.0),
                   reads=[f"sbpln{hh}", "cstt"], writes=[f"ubuf{hh}"],
                   tag=f"exp{ch}_{i}_{hh}")
            for hh in range(2):
                V_(lambda e, hh=hh: e.reduce_sum(
                       lnst[:, 128 + 8 * hh:136 + 8 * hh],
                       ubuf[:, hh * 1024:(hh + 1) * 1024]
                       .rearrange("p (h j) -> p h j", h=8), axis=AX.X),
                   reads=[f"ubuf{hh}"], writes=[f"lnr{hh}"], tag=f"rsum{ch}_{i}_{hh}")
                V_(lambda e, hh=hh: e.reciprocal(lnst[:, 144 + 8 * hh:152 + 8 * hh],
                                                 lnst[:, 128 + 8 * hh:136 + 8 * hh]),
                   reads=[f"lnr{hh}"], writes=[f"lnrec{hh}"], tag=f"rec{ch}_{i}_{hh}")
                V_(lambda e, hh=hh: e.tensor_tensor(
                       ubuf[:, hh * 1024:(hh + 1) * 1024]
                       .rearrange("p (h j) -> p h j", h=8),
                       ubuf[:, hh * 1024:(hh + 1) * 1024]
                       .rearrange("p (h j) -> p h j", h=8),
                       lnst[:, 144 + 8 * hh:152 + 8 * hh].unsqueeze(2)
                       .broadcast_to([128, 8, 128]), ALU.mult),
                   reads=[f"lnrec{hh}"], writes=[f"ubuf{hh}"], tag=f"norm{ch}_{i}_{hh}")

        def ut_av(i):
            par = i % 3
            for hh in range(2):
                ptres = f"pt{hh}"
                for hl in range(8):
                    h = hh * 8 + hl
                    T_(lambda e, hh=hh, hl=hl, h=h:
                       e.transpose(pt[:, hh * 1024 + hl * 128: hh * 1024 + (hl + 1) * 128],
                                   ubuf[:, h * 128:(h + 1) * 128], eye[:]),
                       reads=[f"ubuf{hh}", "eye"], writes=[ptres], tag=f"trU{ch}_{i}_{h}")
                S_(lambda e, hh=hh:
                   e.copy(utb[:, hh * 1024:(hh + 1) * 1024], pt[:, hh * 1024:(hh + 1) * 1024]),
                   reads=[ptres], writes=[f"utb{hh}"], tag=f"utev{ch}_{i}_{hh}")
                for h in [hh * 8 + o for o in (0, 2, 4, 6, 1, 3, 5, 7)]:
                    T_(lambda e, h=h, par=par, hh=hh:
                       e.matmul(p2[64 * (h % 2):64 * (h % 2) + 64,
                                   (h // 2) * 128:(h // 2 + 1) * 128],
                                vtok[par][:, h * 64:(h + 1) * 64],
                                utb[:, h * 128:(h + 1) * 128], start=True, stop=True),
                       reads=[f"vtok_{par}", f"utb{hh}"], writes=["p2"],
                       tag=f"av{ch}_{i}_{h}")
            S_(lambda e, i=i:
               e.copy(
                   attnT[:].rearrange("p (g t) -> p g t", g=8)[:, :, i * 128:(i + 1) * 128],
                   p2[:].rearrange("p (g t) -> p g t", g=8)),
               reads=["p2"], writes=["attnT"], tag=f"avev{ch}_{i}")

        if attn_parts < 4:
            run_conv(9)
            continue
        if attn_parts < 8:
            for _i in range(4):
                v_mm(_i)
                if attn_parts >= 5:
                    s_mm(_i)
                if attn_parts >= 6:
                    softmax(_i)
                if attn_parts >= 7:
                    ut_av(_i)
            run_conv(9)
            continue
        v_mm(0)
        s_mm(0)
        softmax(0)
        v_mm(1)
        s_mm(1)
        ut_av(0)
        softmax(1)
        v_mm(2)
        s_mm(2)
        ut_av(1)
        softmax(2)
        v_mm(3)
        s_mm(3)
        ut_av(2)
        softmax(3)
        ut_av(3)
        run_conv(9)

        if ch == 0 and attn_parts >= 3:
            V_(lambda e: e.tensor_copy(beff[:], bprt[:]),
               reads=["bprt"], writes=["beff"], tag="beffcp")
            for c in range(8):
                cp = c // 2
                buf, bres = (wpA, "wpA") if cp % 2 == 0 else (wpB, "wpB")
                if c % 2 == 0:
                    DMA("S", f"wp{cp % 2}", buf[:],
                        bass.AP(wpscr.tensor, 2 * cp * 128 * 1024,
                                [[1024, 128], [128 * 1024, 2], [1, 1024]]),
                        deps=[markers["wp"]], writes=[bres], tag=f"wpld_beff{cp}")
                for half in range(2):
                    T_(lambda e, buf=buf, bres=bres, c=c, half=half:
                       e.matmul(p2[0:1, half * 512:(half + 1) * 512], bqvb[:, c:c + 1],
                                buf[:, (c % 2) * 1024 + half * 512:
                                    (c % 2) * 1024 + (half + 1) * 512],
                                start=(c == 0), stop=(c == 7)),
                       reads=[bres, "bqvb"], writes=["p2"], tag=f"beffmm{c}_{half}")
            S_(lambda e: e.copy(bvwp[:], p2[0:1, :]), reads=["p2"], writes=["bvwp"],
               tag="beffev")
            for half in range(2):
                T_(lambda e, half=half:
                   e.matmul(p2[:, half * 512:(half + 1) * 512], ones1[:],
                            bvwp[0:1, half * 512:(half + 1) * 512], start=True, stop=True),
                   reads=["bvwp", "ones1"], writes=["p2"], tag=f"beffbc{half}")
            V_(lambda e: e.tensor_add(beff[:], p2[:], beff[:]),
               reads=["p2", "beff"], writes=["beff"], tag="beffadd")


        ptf = pt[:].bitcast(F32)
        pjdsts = [pS[:, 0:1024], pS[:, 1024:2048], p2[:], ptf]
        pjres = [["pS0", "pS1"], ["pS2", "pS3"], ["p2"], ["pt0", "pt1"]]
        if attn_parts >= 9:
            for c in range(8):
                cp = c // 2
                buf, bres = (wpA, "wpA") if cp % 2 == 0 else (wpB, "wpB")
                if c % 2 == 0:
                    DMA("S", f"wp{cp % 2}", buf[:],
                        bass.AP(wpscr.tensor, 2 * cp * 128 * 1024,
                                [[1024, 128], [128 * 1024, 2], [1, 1024]]),
                        deps=[markers["wp"]], writes=[bres], tag=f"wpld{ch}_{cp}")
                for i in range(4):
                    for half in range(2):
                        T_(lambda e, buf=buf, bres=bres, c=c, i=i, half=half:
                           e.matmul(pjdsts[i][:, half * 512:(half + 1) * 512],
                                    attnT[:, c * 512 + i * 128: c * 512 + (i + 1) * 128],
                                    buf[:, (c % 2) * 1024 + half * 512:
                                        (c % 2) * 1024 + (half + 1) * 512],
                                    start=(c == 0), stop=(c == 7)),
                           reads=["attnT", bres],
                           writes=pjres[i], tag=f"pj{ch}_{i}_{c}_{half}")
            pSb = pS[:].bitcast(BF16)
            atf = attnT[:].bitcast(F32)
            pstage = [sbuf_sm[:, 0:1024], sbuf_sm[:, 1024:2048],
                      atf[:, 0:1024], atf[:, 1024:2048]]
            psres = ["sbpln0", "sbpln1", "attnT", "attnT"]
            for i in range(4):
                V_(lambda e, i=i:
                   e.scalar_tensor_tensor(pstage[i], pjdsts[i], 1.0,
                                          beff[:], ALU.mult, ALU.add),
                   reads=pjres[i] + ["beff"], writes=[psres[i]],
                   tag=f"pjev{ch}_{i}")
                if ch + 1 < n_attn:
                    ln1_tr(ch + 1, i, pSb[:, i * 1024:(i + 1) * 1024], [f"pS{i}"])
            if ch + 1 == n_attn and n_ffn:
                ln3_block(0)
                ln3_done.append(1)
            for i in range(4):
                tb = tbs[i]
                rstd, bv, sres = ln_chain(
                    lambda hf, i=i: pstage[i][:, hf * 512:(hf + 1) * 512], i,
                    psres[i], f"ln2_{tb}",
                    scratch=ubuf[:].bitcast(F32)[:, 512:1024], scres="ubuf")
                V_(lambda e, i=i, rstd=rstd, bv=bv:
                   e.tensor_scalar(pstage[i], pstage[i], rstd, bv, ALU.mult, ALU.add),
                   reads=[sres], writes=[psres[i]], tag=f"ln2ap{tb}")
                V_(lambda e, tb=tb, i=i:
                   e.tensor_add(xbuf[:, tb * D:(tb + 1) * D], xbuf[:, tb * D:(tb + 1) * D],
                                pstage[i]),
                   reads=[psres[i]], writes=[f"x{tb}"], tag=f"res2_{tb}")
        run_conv(9)

    if n_ffn and not ln3_done:
        ln3_block(0)
    for ch in range(n_ffn):
        tbs = [4 * ch + i for i in range(4)]
        lpar = 0

        def w0_load(j, ch=ch):
            buf, bres = (w0A, "w0A") if j % 2 == 0 else (w0B, "w0B")
            qeng = "S" if (ch == 0 and j < 4) else "Y"
            DMA(qeng, f"w0b{j % 2}", buf[:].rearrange("p (c j) -> p c j", c=8),
                bass.AP(w0scr.tensor, j * 512,
                        [[F0, 128], [128 * F0, 8], [1, 512]]),
                deps=[markers["w0"]], writes=[bres], tag=f"w0ld{ch}_{j}")

        for j in range(8):
            buf, bres = (w0A, "w0A") if j % 2 == 0 else (w0B, "w0B")
            if j >= 2 or ch == 0:
                w0_load(j)
            for m in range(4):
                f0b = 4 * j + m
                slot = f0b % 4
                for c in range(8):
                    T_(lambda e, buf=buf, bres=bres, m=m, c=c, slot=slot, lpar=lpar:
                       e.matmul(pS[:, slot * 512:(slot + 1) * 512],
                                buf[:, c * 512 + m * 128: c * 512 + m * 128 + 128],
                                ln3T[lpar][:, c * 512:(c + 1) * 512],
                                start=(c == 0), stop=(c == 7)),
                       reads=[bres, f"ln3T_{lpar}"], writes=[f"pS{slot}"],
                       tag=f"l0_{ch}_{f0b}_{c}")
                if m % 2 == 1:
                    fp = f0b // 2
                    pr = fp % 2
                    S_(lambda e, fp=fp, pr=pr:
                       e.activation(g0T[:, 2 * fp * 512:(2 * fp + 2) * 512],
                                    pS[:, pr * 1024:(pr + 1) * 1024], AF.Relu,
                                    bias=0.0, scale=sv2[:, 0:1]),
                       reads=[f"pS{pr * 2}", f"pS{pr * 2 + 1}", "sv2"], writes=["g0T"],
                       tag=f"g0ev{ch}_{fp}")
                    if fp == 0 and ch + 1 < n_ffn:
                        ln3_sb.update(ln_stats4(ch + 1, "ln3_"))

        def w2_load(j, ch=ch):
            buf, bres = (w2A, "w2A") if j % 2 == 0 else (w2B, "w2B")
            DMA("Y", f"w2b{j % 2}", buf[:].rearrange("p (b j) -> p b j", b=4),
                bass.AP(w2scr.tensor, 4 * j * 128 * 1024,
                        [[1024, 128], [128 * 1024, 4], [1, 1024]]),
                deps=[markers["w2"]], writes=[bres], tag=f"w2ld{ch}_{j}")

        w2_load(0)
        w2_load(1)
        if ch + 1 < n_ffn:
            for _i in range(4):
                ln3_tr(ch + 1, _i)
        if ch + 1 < n_ffn:
            w0_load(0, ch + 1)
            w0_load(1, ch + 1)

        for s in range(16):
            wb = s % 2
            DMA("Y", f"w1b{wb}",
                w1b[wb][:].rearrange("p (c j) -> p c j", c=32),
                bass.AP(w1scr.tensor, s * 256,
                        [[F1, 128], [128 * F1, 32], [1, 256]]),
                deps=[markers["w1"]],
                writes=[f"w1b_{wb}q{g}" for g in range(4)], tag=f"w1ld{ch}_{s}")
            base = 2 * (s % 2)
            for c in range(32):
                for fl in range(2):
                    T_(lambda e, wb=wb, c=c, fl=fl, base=base:
                       e.matmul(pS[:, (base + fl) * 512:(base + fl + 1) * 512],
                                w1b[wb][:, c * 256 + fl * 128: c * 256 + fl * 128 + 128],
                                g0T[:, c * 512:(c + 1) * 512],
                                start=(c == 0), stop=(c == 31)),
                       reads=[f"w1b_{wb}q{c // 8}", "g0T"], writes=[f"pS{base + fl}"],
                       tag=f"w1_{ch}_{s}_{c}_{fl}")
            S_(lambda e, s=s, base=base:
               e.activation(g1T[:, s * 1024:(s + 1) * 1024],
                            pS[:, (base // 2) * 1024:(base // 2 + 1) * 1024], AF.Relu,
                            bias=0.0, scale=sv2[:, 1:2]),
               reads=[f"pS{base}", f"pS{base + 1}", "sv2"], writes=["g1T"],
               tag=f"g1ev{ch}_{s}")


        ptf = pt[:].bitcast(F32)
        w2dsts = [pS[:, 0:1024], pS[:, 1024:2048], p2[:], ptf]
        w2res = [["pS0", "pS1"], ["pS2", "pS3"], ["p2"], ["pt0", "pt1"]]
        for fcp in range(16):
            j = fcp // 2
            buf, bres = (w2A, "w2A") if j % 2 == 0 else (w2B, "w2B")
            q = fcp % 2
            if fcp >= 4 and fcp % 2 == 0:
                w2_load(fcp // 2)
            for fl in range(2):
                fc = 2 * fcp + fl
                for i in range(4):
                    for half in range(2):
                        T_(lambda e, buf=buf, bres=bres, q=q, fl=fl, fc=fc, half=half, i=i:
                           e.matmul(w2dsts[i][:, half * 512:(half + 1) * 512],
                                    g1T[:, fc * 512 + i * 128: fc * 512 + (i + 1) * 128],
                                    buf[:, q * 2048 + fl * 1024 + half * 512:
                                        q * 2048 + fl * 1024 + (half + 1) * 512],
                                    start=(fc == 0), stop=(fc == 31)),
                           reads=[bres, "g1T"], writes=w2res[i],
                           tag=f"w2_{ch}_{i}_{fc}_{half}")
        hbufs = [(hbuf[:], "hbuf"), (w2A[:].bitcast(F32)[:, 0:1024], "w2A")]

        def h2ev(i, ch=ch):
            hb, hres = hbufs[i % 2]
            V_(lambda e, i=i, hb=hb:
               e.tensor_scalar(hb, w2dsts[i], sv2[:, 2:3], None, ALU.mult),
               reads=w2res[i] + ["sv2"], writes=[hres], tag=f"h2ev{ch}_{i}")

        def ln4_fin(i):
            tb = tbs[i]
            hb, hres = hbufs[i % 2]
            rstd, bv, sres = ln_chain(
                lambda half, hb=hb: hb[:, half * 512:(half + 1) * 512], i, hres,
                f"ln4_{tb}",
                scratch=lnx3[0][:].bitcast(F32)[:, 0:512], scres="lnx3_0")
            V_(lambda e, hb=hb, rstd=rstd, bv=bv:
               e.tensor_scalar(hb, hb, rstd, bv, ALU.mult, ALU.add),
               reads=[sres], writes=[hres], tag=f"ln4ap{tb}")
            V_(lambda e, tb=tb, hb=hb:
               e.tensor_add(xbuf[:, tb * D:(tb + 1) * D], xbuf[:, tb * D:(tb + 1) * D],
                            hb),
               reads=[hres], writes=[f"x{tb}"], tag=f"res4_{tb}")
            DMA("S", "outq", out[tb * 128:(tb + 1) * 128, :], xbuf[:, tb * D:(tb + 1) * D],
                reads=[f"x{tb}"], tag=f"st{tb}")

        h2ev(0)
        h2ev(1)
        ln4_fin(0)
        h2ev(2)
        ln4_fin(1)
        h2ev(3)
        ln4_fin(2)
        ln4_fin(3)

    assert (not do_conv) or n_attn < NCH or conv_pos[0] == len(conv_jobs), \
        f"conv jobs left: {len(conv_jobs) - conv_pos[0]}"
    if do_conv and n_attn == NCH:
        assert set(markers) == {"wp", "w0", "w1", "w2"}, markers.keys()

    P.finalize(sem_handles)

    def make_body(ename):
        stream = P.streams[ename]
        csem = sem_handles["c" + ename]

        def body(eng):
            for op in stream:
                for (sem, val) in op.waits:
                    eng.wait_ge(sem, val)
                if op.emit is None:
                    continue
                inst = op.emit(eng)
                if op.is_dma:
                    inst.then_inc(sem_handles[op.semkey], 16)
                elif op.need_inc:
                    inst.then_inc(csem, 1)
        return body

    fin = Op("Y", None, False, "fin")
    fin.waits = []
    for k, v in P.totals["d"].items():
        fin.waits.append((sem_handles[k], v))
    for e in ENGINES:
        if P.totals["c"][e]:
            fin.waits.append((sem_handles["c" + e], P.totals["c"][e]))
    P.streams["Y"].append(fin)

    with nc.Block() as block:
        block.tensor(make_body("T"))
        block.vector(make_body("V"))
        block.scalar(make_body("S"))
        block.gpsimd(make_body("G"))
        block.sync(make_body("Y"))
    st.close()
    return nc


_NC_CACHE = []


def kernel(L0_attn_x, L0_attn_w_qkv, L0_attn_b_qkv, L0_attn_w_proj, L0_attn_b_proj,
           L0_attn_bias_table, L0_ffn_k0, L0_ffn_k1, L0_ffn_k2, L0_ffn_s0, L0_ffn_s1,
           L0_ffn_s2):
    x = np.ascontiguousarray(np.asarray(L0_attn_x, np.float32))
    wqkv = np.ascontiguousarray(np.asarray(L0_attn_w_qkv, np.float32))
    wproj = np.ascontiguousarray(np.asarray(L0_attn_w_proj, np.float32))
    bqkv = np.asarray(L0_attn_b_qkv, np.float32)
    bproj = np.asarray(L0_attn_b_proj, np.float32)
    btab = np.asarray(L0_attn_bias_table, np.float32)
    k0 = np.ascontiguousarray(np.asarray(L0_ffn_k0).astype(np.uint8))
    k1 = np.ascontiguousarray(np.asarray(L0_ffn_k1).astype(np.uint8))
    k2 = np.ascontiguousarray(np.asarray(L0_ffn_k2).astype(np.uint8))
    s0 = np.float32(L0_ffn_s0)
    s1 = np.float32(L0_ffn_s1)
    s2 = np.float32(L0_ffn_s2)

    idx = _rel_index()
    bias = btab[idx.reshape(-1), :].reshape(WS, WS, H)
    ball = np.full((128, 2048), NEG, np.float32)
    for h in range(H):
        for a in range(16):
            ball[a * 8:(a + 1) * 8, h * 128 + a * 8: h * 128 + (a + 1) * 8] = bias[:, :, h]

    bqt = np.ascontiguousarray(bqkv.reshape(24, 128).T)
    bpr = np.ascontiguousarray(np.broadcast_to(bproj[None, :], (128, 1024)))
    sv = np.ascontiguousarray(np.broadcast_to(
        np.stack([s0, s1, s2])[None, :], (128, 3))).astype(np.float32)
    eyeb = np.eye(128, dtype=np.float32).astype(ml_dtypes.bfloat16)
    oneb = np.ones((1, 128), ml_dtypes.bfloat16)
    cst = np.ascontiguousarray(np.broadcast_to(
        np.array([EPS, -20.0], np.float32)[None, :], (128, 2)))

    rows = []
    in_maps = []
    for core in range(NCORES):
        b, r = divmod(core, 4)
        rr = (np.arange(2048 * r, 2048 * (r + 1)) + SHIFT) % SEQ
        rows.append((b, rr))
        xs = np.ascontiguousarray(x[b, rr, :])
        in_maps.append(dict(xs=xs, wqkv=wqkv, wproj=wproj, k0=k0, k1=k1, k2=k2,
                            ball=ball, bqt=bqt, bpr=bpr, svin=sv, eyein=eyeb,
                            onein=oneb, cstin=cst))

    if not _NC_CACHE:
        _NC_CACHE.append(build_program())
    nc = _NC_CACHE[0]
    res = run_bass_kernel_spmd(nc, in_maps, core_ids=list(range(NCORES)))

    outp = np.empty((2, SEQ, D), np.float32)
    for core in range(NCORES):
        b, rr = rows[core]
        outp[b, rr, :] = np.asarray(res.results[core]["out"], np.float32)
    return outp


# version 43 (final)
# speedup vs baseline: 1.0169x; 1.0024x over previous
import numpy as np
import ml_dtypes
from contextlib import ExitStack

import concourse.bass as bass
import concourse.mybir as mybir
from concourse.bass_utils import run_bass_kernel_spmd

F32 = mybir.dt.float32
BF16 = mybir.dt.bfloat16
U8 = mybir.dt.uint8
AF = mybir.ActivationFunctionType
ALU = mybir.AluOpType
AX = mybir.AxisListType

NCORES = 8
T = 2048
NTB = 16
NCH = 4
D = 1024
H = 16
HD = 64
WS = 8
SHIFT = 4
SEQ = 8192
F0 = 4096
F1 = 4096
NEG = -30000.0
EPS = 1e-5

ENGINES = ("T", "V", "S", "G", "Y")


class Op:
    __slots__ = ("eng", "emit", "deps", "is_dma", "need_inc", "val", "waits", "tag",
                 "semkey", "dma_waits")

    def __init__(self, eng, emit, is_dma, tag="", semkey=None):
        self.eng = eng
        self.emit = emit
        self.deps = []
        self.is_dma = is_dma
        self.need_inc = False
        self.val = None
        self.waits = None
        self.tag = tag
        self.semkey = semkey
        self.dma_waits = {}


class Prog:
    def __init__(self):
        self.streams = {e: [] for e in ENGINES}
        self.order = []
        self.res = {}
        self.overlays = {}
        self.dmacum = {}

    def _dep(self, op, d):
        if d is None or d is op:
            return
        if d.eng == op.eng and not d.is_dma and not op.is_dma:
            if d.eng == "T":
                return
        if d.is_dma:
            if d.is_dma and op.is_dma and d.eng == op.eng and d.semkey == op.semkey:
                return
            k = d.semkey
            v = self.dmacum[k]
            if op.dma_waits.get(k, 0) < v:
                op.dma_waits[k] = v
            return
        op.deps.append(d)
        d.need_inc = True

    def record(self, eng, emit, reads=(), writes=(), deps=(), is_dma=False, tag="",
               semkey=None):
        op = Op(eng, emit, is_dma, tag, semkey)
        for r in reads:
            e = self.res.get(r)
            if e is not None and e[0] is not None:
                self._dep(op, e[0])
        for w in writes:
            e = self.res.get(w)
            if e is not None:
                if e[0] is not None:
                    self._dep(op, e[0])
                for rd in e[1]:
                    self._dep(op, rd)
            elif w in self.overlays:
                for o in self.overlays[w]:
                    oe = self.res.get(o)
                    if oe is not None:
                        if oe[0] is not None:
                            self._dep(op, oe[0])
                        for rd in oe[1]:
                            self._dep(op, rd)
        for d in deps:
            self._dep(op, d)
        if is_dma:
            self.dmacum[semkey] = self.dmacum.get(semkey, 0) + 16
        for r in reads:
            self.res.setdefault(r, [None, []])[1].append(op)
        for w in writes:
            self.res[w] = [op, []]
        self.streams[eng].append(op)
        self.order.append(op)
        return op

    def finalize(self, sems):
        ctr = {e: 0 for e in ENGINES}
        for op in self.order:
            if (not op.is_dma) and op.need_inc:
                ctr[op.eng] += 1
                op.val = ctr[op.eng]
        for e in ENGINES:
            waited = {}
            for op in self.streams[e]:
                ws = []
                for k, v in op.dma_waits.items():
                    if waited.get(k, -1) < v:
                        waited[k] = v
                        ws.append((sems[k], v))
                for d in op.deps:
                    key = "c" + d.eng
                    v = d.val
                    assert v is not None, f"dep without value: {d.tag}"
                    if waited.get(key, -1) < v:
                        waited[key] = v
                        ws.append((sems[key], v))
                op.waits = ws
        self.totals = dict(c=ctr, d=self.dmacum)


def _rel_index():
    coords = np.stack(np.meshgrid(np.arange(WS), np.arange(WS), indexing="ij"))
    cf = coords.reshape(2, -1)
    rel = cf[:, :, None] - cf[:, None, :]
    rel = rel.transpose(1, 2, 0).astype(np.int64)
    rel[..., 0] += WS - 1
    rel[..., 1] += WS - 1
    rel[..., 0] *= 2 * WS - 1
    return rel.sum(-1)[:WS, :WS]


def build_program(n_attn=NCH, n_ffn=NCH, do_conv=True, attn_parts=9):
    nc = bass.Bass("TRN2", target_bir_lowering=False)

    xs = nc.dram_tensor("xs", [T, D], F32, kind="ExternalInput").ap()
    wqkv = nc.dram_tensor("wqkv", [D, 3 * D], F32, kind="ExternalInput").ap()
    wproj = nc.dram_tensor("wproj", [D, D], F32, kind="ExternalInput").ap()
    k0 = nc.dram_tensor("k0", [2 * D, F0], U8, kind="ExternalInput").ap()
    k1 = nc.dram_tensor("k1", [F0, F1], U8, kind="ExternalInput").ap()
    k2 = nc.dram_tensor("k2", [F1, D], U8, kind="ExternalInput").ap()
    ball = nc.dram_tensor("ball", [128, 2048], F32, kind="ExternalInput").ap()
    bqt = nc.dram_tensor("bqt", [128, 24], F32, kind="ExternalInput").ap()
    bpr = nc.dram_tensor("bpr", [128, D], F32, kind="ExternalInput").ap()
    svin = nc.dram_tensor("svin", [128, 3], F32, kind="ExternalInput").ap()
    eyein = nc.dram_tensor("eyein", [128, 128], BF16, kind="ExternalInput").ap()
    onein = nc.dram_tensor("onein", [1, 128], BF16, kind="ExternalInput").ap()
    cstin = nc.dram_tensor("cstin", [128, 2], F32, kind="ExternalInput").ap()
    out = nc.dram_tensor("out", [T, D], F32, kind="ExternalOutput").ap()

    w1scr = nc.dram_tensor("w1scr", [F0, F1], BF16, kind="Internal").ap()
    w0scr = nc.dram_tensor("w0scr", [D, F0], BF16, kind="Internal").ap()
    w2scr = nc.dram_tensor("w2scr", [32, 128, 1024], BF16, kind="Internal").ap()
    wpscr = nc.dram_tensor("wpscr", [8, 128, 1024], BF16, kind="Internal").ap()

    BASE = 16512
    cur = [BASE]

    ranges = {}

    def alloc(name, shape, dt):
        per = int(np.prod(shape[1:])) * mybir.dt.size(dt)
        per = (per + 31) // 32 * 32
        t = nc.alloc_sbuf_tensor_at(name, shape, dt, offset=cur[0])
        ranges[name] = (cur[0], cur[0] + per)
        cur[0] += per
        return t

    def alloc_at(name, shape, dt, off):
        return nc.alloc_sbuf_tensor_at(name, shape, dt, offset=off)

    xbuf = alloc("xbuf", [128, NTB * D], F32)
    ballt = alloc("ballt", [128, 2048], F32)
    bqtt = alloc("bqtt", [128, 24], F32)
    bqvb = alloc("bqvb", [128, 8], BF16)
    svt = alloc("svt", [128, 3], F32)
    sv2 = alloc("sv2", [128, 3], F32)
    eye = alloc("eye", [128, 128], BF16)
    ones1 = alloc("ones1", [1, 128], BF16)
    bprt = alloc("bprt", [128, D], F32)
    beff = alloc("beff", [128, D], F32)
    lnst = alloc("lnst", [128, 160], F32)
    hbuf = alloc("hbuf", [128, D], F32)
    bvwp = alloc_at("bvwp", [1, 1024], BF16, ranges["hbuf"][0])
    cstt = alloc("cstt", [128, 2], F32)

    PH = cur[0]
    cur[0] = PH
    wqkvb = alloc("wqkvb", [128, 8 * 3072], BF16)
    lnx1 = [alloc(f"lnx1_{i}", [128, D], BF16) for i in range(2)]
    lnx1T = alloc("lnx1T", [128, 8 * 512], BF16)
    qkT = alloc("qkT", [128, 16 * 512], BF16)
    vtok = [alloc(f"vtok_{i}", [128, D], BF16) for i in range(3)]
    attnT = alloc("attnT", [128, 8 * 512], BF16)
    sbuf_sm = alloc("sbpln", [128, 2048], F32)
    ubuf = alloc("ubuf", [128, 2048], BF16)
    ranges["sbpln0"] = (ranges["sbpln"][0], ranges["sbpln"][0] + 4096)
    ranges["sbpln1"] = (ranges["sbpln"][0] + 4096, ranges["sbpln"][1])
    ranges["ubuf0"] = (ranges["ubuf"][0], ranges["ubuf"][0] + 2048)
    ranges["ubuf1"] = (ranges["ubuf"][0] + 2048, ranges["ubuf"][1])
    utb = alloc("utb", [128, 2048], BF16)
    ranges["utb0"] = (ranges["utb"][0], ranges["utb"][0] + 2048)
    ranges["utb1"] = (ranges["utb"][0] + 2048, ranges["utb"][1])
    wp = [alloc(f"wp_{i}", [128, D], BF16) for i in range(2)]
    cvb = alloc("cvb", [128, 4096], BF16)
    cv = [cvb[:, 0:2048], cvb[:, 2048:4096]]
    ranges["cv_0"] = (ranges["cvb"][0], ranges["cvb"][0] + 4096)
    ranges["cv_1"] = (ranges["cvb"][0] + 4096, ranges["cvb"][1])
    cvu8 = [alloc(f"cu_{i}", [128, 2048], U8) for i in range(2)]
    attn_end = cur[0]

    cur[0] = PH
    w1b = [alloc(f"w1b_{i}", [128, 32 * 256], BF16) for i in range(2)]
    w0b = [alloc(f"w0b_{i}", [128, 2048], BF16) for i in range(2)]
    w2b = [alloc(f"w2b_{i}", [128, 2048], BF16) for i in range(2)]
    ln3T = [alloc("ln3T_0", [128, 8 * 512], BF16)]
    _l3 = alloc("lnx3_0", [128, D], BF16)
    lnx3 = [_l3, _l3]
    g0T = alloc("g0T", [128, 32 * 512], BF16)
    g1T = alloc("g1T", [128, 32 * 512], BF16)
    ffn_end = cur[0]
    w0A = alloc_at("w0A", [128, 4096], BF16, ranges["w0b_0"][0])
    ranges["w0A"] = (ranges["w0b_0"][0], ranges["w0b_1"][1])
    w0B = alloc_at("w0B", [128, 4096], BF16, ranges["ballt"][0])
    ranges["w0B"] = ranges["ballt"]
    w2A = alloc_at("w2A", [128, 4096], BF16, ranges["w2b_0"][0])
    ranges["w2A"] = (ranges["w2b_0"][0], ranges["w2b_1"][1])
    w2B = alloc_at("w2B", [128, 4096], BF16, ranges["bprt"][0])
    ranges["w2B"] = (ranges["bprt"][0], ranges["beff"][1])
    wpA = alloc_at("wpA", [128, 2048], BF16, ranges["wp_0"][0])
    ranges["wpA"] = (ranges["wp_0"][0], ranges["wp_1"][1])
    wpB = alloc_at("wpB", [128, 2048], BF16, ranges["bprt"][0])
    ranges["wpB"] = ranges["bprt"]

    LIMIT = 229376 - 192
    assert attn_end <= LIMIT, f"attn region overflow {attn_end}"
    assert ffn_end <= LIMIT, f"ffn region overflow {ffn_end}"

    ATTN_NAMES = ["wqkvb", "lnx1_0", "lnx1_1", "lnx1T", "qkT", "vtok_0", "vtok_1",
                  "vtok_2", "attnT", "sbpln0", "sbpln1", "ubuf0", "ubuf1",
                  "utb0", "utb1", "wp_0", "wp_1",
                  "cv_0", "cv_1", "cu_0", "cu_1"]
    FFN_NAMES = ["w1b_0", "w1b_1", "w0A", "w2A", "ln3T_0",
                 "lnx3_0", "g0T", "g1T"]

    def _overlaps(a, b):
        (a0, a1), (b0, b1) = ranges[a], ranges[b]
        return a0 < b1 and b0 < a1

    P = Prog()
    for _fb in FFN_NAMES:
        P.overlays[_fb] = [a for a in ATTN_NAMES if _overlaps(_fb, a)]
    P.overlays["hbuf"] = ["bvwp"]
    P.overlays["w0B"] = ["ballt"]
    P.overlays["w2B"] = ["bprt", "beff", "wpB"]
    P.overlays["wpB"] = ["bprt"]

    st = ExitStack()
    pS = st.enter_context(nc.psum_tensor("pS", [128, 2048], F32))
    p2 = st.enter_context(nc.psum_tensor("p2", [128, 1024], F32))
    pt = st.enter_context(nc.psum_tensor("pt", [128, 2048], BF16))
    SEMKEYS = ("cT", "cV", "cS", "cG", "cY", "setup", "wqkvc", "cv0", "cv1", "cu0", "cu1",
               "sw0", "sw1", "wp0", "wp1", "w0b0", "w0b1",
               "w1b0", "w1b1", "w2b0", "w2b1", "outq")
    sem_handles = {}
    for name in SEMKEYS:
        sem_handles[name] = st.enter_context(nc.semaphore(name))

    def T_(emit, reads=(), writes=(), deps=(), tag=""):
        return P.record("T", emit, reads, writes, deps, tag=tag)

    def V_(emit, reads=(), writes=(), deps=(), tag=""):
        return P.record("V", emit, reads, writes, deps, tag=tag)

    def S_(emit, reads=(), writes=(), deps=(), tag=""):
        return P.record("S", emit, reads, writes, deps, tag=tag)

    def G_(emit, reads=(), writes=(), deps=(), tag=""):
        return P.record("G", emit, reads, writes, deps, tag=tag)

    def DMA(eng, semkey, outap, inap, reads=(), writes=(), deps=(), tag=""):
        return P.record(eng, (lambda e, o=outap, i=inap: e.dma_start(o, i)),
                        reads, writes, deps, is_dma=True, tag=tag, semkey=semkey)

    ld = {}
    for tb in range(NTB):
        ld[f"x{tb}"] = DMA("Y", "setup", xbuf[:, tb * D:(tb + 1) * D], xs[tb * 128:(tb + 1) * 128, :],
                           writes=[f"x{tb}"], tag=f"ldx{tb}")
    DMA("Y", "setup", ballt[:], ball, writes=["ballt"])
    DMA("Y", "setup", bqtt[:], bqt, writes=["bqtt"])
    DMA("Y", "setup", svt[:], svin, writes=["svt"])
    DMA("Y", "setup", eye[:], eyein, writes=["eye"])
    DMA("Y", "setup", ones1[:], onein, writes=["ones1"])
    DMA("Y", "setup", bprt[:], bpr, writes=["bprt"])
    DMA("Y", "setup", cstt[:], cstin, writes=["cstt"])

    for c in range(8):
        DMA("G", "wqkvc", wqkvb[:, c * 3072:(c + 1) * 3072], wqkv[c * 128:(c + 1) * 128, :],
            writes=["wqkvb"], tag=f"wqkvcast{c}")

    G_(lambda e: e.tensor_scalar(sv2[:, 0:1], svt[:, 0:1], 4.0, None, ALU.mult),
       reads=["svt"], writes=["sv2"])
    G_(lambda e: e.tensor_scalar(sv2[:, 1:2], svt[:, 1:2], 2.0, None, ALU.mult),
       reads=["svt"], writes=["sv2"])
    G_(lambda e: e.tensor_copy(sv2[:, 2:3], svt[:, 2:3]), reads=["svt"], writes=["sv2"])
    G_(lambda e: e.tensor_copy(bqvb[:], bqtt[:, 16:24]), reads=["bqtt"], writes=["bqvb"])

    def mk_conv_jobs():
        jobs = []
        for c in range(8):
            jobs.append(("wp", c, 0))
        for c in range(32):
            for h in range(2):
                jobs.append(("w1", c, h))
        for c in range(8):
            for q in range(4):
                jobs.append(("w0", c, q))
        for c in range(32):
            jobs.append(("w2", c, 0))
        return jobs

    def _convkey(j):
        return j[0]

    conv_jobs = mk_conv_jobs()
    conv_last = {}
    for _i, _j in enumerate(conv_jobs):
        conv_last[_convkey(_j)] = _i
    markers = {}

    def conv_cast(i):
        kind, a, b2 = conv_jobs[i]
        b = i % 2
        if kind == "wp":
            DMA("G", f"cv{b}", cv[b][:, 0:1024], wproj[a * 128:(a + 1) * 128, :],
                writes=[f"cv_{b}"], tag=f"wpcast{a}")
        elif kind == "w2":
            DMA("Y", f"cu{b}", cvu8[b][:, 0:1024], k2[a * 128:(a + 1) * 128, :],
                writes=[f"cu_{b}"], tag=f"w2c{a}")
        elif kind == "w0":
            DMA("Y", f"cu{b}", cvu8[b][:].rearrange("p (h j) -> p h j", h=2),
                bass.AP(k0.tensor, a * 128 * F0 + b2 * 1024,
                        [[F0, 128], [D * F0, 2], [1, 1024]]),
                writes=[f"cu_{b}"], tag=f"w0c{a}_{b2}")
        else:
            DMA("Y", f"cu{b}", cvu8[b][:],
                k1[a * 128:(a + 1) * 128, b2 * 2048:(b2 + 1) * 2048],
                writes=[f"cu_{b}"], tag=f"w1c{a}_{b2}")

    def conv_work(i):
        kind, a, b2 = conv_jobs[i]
        b = i % 2
        if kind == "wp":
            DMA("G", f"sw{b}", wpscr[a], cv[b][:, 0:1024], reads=[f"cv_{b}"],
                writes=[f"wpscr_w{b}"], tag=f"wpw{a}")
        elif kind == "w2":
            G_(lambda e, b=b: e.tensor_scalar(cv[b][:, 0:1024], cvu8[b][:, 0:1024],
                                              2.0, -1.0, ALU.mult, ALU.add),
               reads=[f"cu_{b}"], writes=[f"cv_{b}"], tag=f"w2aff{a}")
            if a % 2 == 1:
                DMA("G", "sw0",
                    bass.AP(w2scr.tensor, (a - 1) * 128 * 1024,
                            [[1024, 128], [128 * 1024, 2], [1, 1024]]),
                    cvb[:].rearrange("p (b j) -> p b j", b=2)[:, :, 0:1024],
                    reads=["cv_0", "cv_1"], writes=["w2scr_w0"], tag=f"w2w{a}")
        elif kind == "w0":
            G_(lambda e, b=b: e.tensor_tensor(cv[b][:, 0:1024], cvu8[b][:, 0:1024],
                                              cvu8[b][:, 1024:2048], ALU.subtract),
               reads=[f"cu_{b}"], writes=[f"cv_{b}"], tag=f"w0sub{a}_{b2}")
            if b2 % 2 == 1:
                DMA("G", "sw0",
                    w0scr[a * 128:(a + 1) * 128, (b2 - 1) * 1024:(b2 + 1) * 1024]
                    .rearrange("p (b j) -> p b j", b=2),
                    cvb[:].rearrange("p (b j) -> p b j", b=2)[:, :, 0:1024],
                    reads=["cv_0", "cv_1"], writes=["w0scr_w0"], tag=f"w0w{a}_{b2}")
        else:
            G_(lambda e, b=b: e.tensor_scalar(cv[b][:], cvu8[b][:], 2.0, -1.0,
                                              ALU.mult, ALU.add),
               reads=[f"cu_{b}"], writes=[f"cv_{b}"], tag=f"w1aff{a}_{b2}")
            if b2 == 1:
                DMA("G", "sw0", w1scr[a * 128:(a + 1) * 128, :], cvb[:],
                    reads=["cv_0", "cv_1"], writes=["w1scr_w0"], tag=f"w1w{a}")
        key = _convkey(conv_jobs[i])
        if conv_last[key] == i:
            rd = {"wp": ["wpscr_w0", "wpscr_w1"], "w1": ["w1scr_w0"],
                  "w0": ["w0scr_w0"], "w2": ["w2scr_w0"]}[key]
            markers[key] = G_(lambda e: e.engine_nop(), reads=rd, tag=f"{key}mark")

    conv_pos = [0, 0]

    def run_conv(n):
        if not do_conv:
            return
        k = 0
        while k < n and conv_pos[0] < len(conv_jobs):
            while conv_pos[1] <= conv_pos[0] + 1 and conv_pos[1] < len(conv_jobs):
                conv_cast(conv_pos[1])
                conv_pos[1] += 1
            conv_work(conv_pos[0])
            conv_pos[0] += 1
            k += 1

    def ln_chain(src_ap_fn, slot, res_src, tag, scratch=None, scres=None):
        c0 = slot * 16
        stat = lnst
        sres = f"lnst{slot}"
        V_(lambda e: e.bn_stats(stat[:, c0 + 4:c0 + 10], src_ap_fn(0)),
           reads=[res_src], writes=[sres], tag=tag + "bn0")
        V_(lambda e: e.bn_stats(stat[:, c0 + 10:c0 + 16], src_ap_fn(1)),
           reads=[res_src], writes=[sres], tag=tag + "bn1")
        V_(lambda e: e.bn_aggr(stat[:, c0:c0 + 2], stat[:, c0 + 4:c0 + 16]),
           reads=[], writes=[sres], tag=tag + "ag")
        S_(lambda e: e.activation(stat[:, c0 + 2:c0 + 3], stat[:, c0 + 1:c0 + 2], AF.Ln,
                                  bias=cstt[:, 0:1], scale=1.0),
           reads=["cstt"], writes=[sres], tag=tag + "ln")
        S_(lambda e: e.activation(stat[:, c0 + 3:c0 + 4], stat[:, c0 + 2:c0 + 3], AF.Exp,
                                  bias=0.0, scale=-0.5),
           reads=[], writes=[sres], tag=tag + "ex")
        V_(lambda e: e.tensor_scalar(stat[:, c0 + 4:c0 + 5], stat[:, c0:c0 + 1],
                                     stat[:, c0 + 3:c0 + 4], -1.0, ALU.mult, ALU.mult),
           reads=[], writes=[sres], tag=tag + "b")
        return stat[:, c0 + 3:c0 + 4], stat[:, c0 + 4:c0 + 5], sres

    def ln_stats4(ch, tagp):
        out = {}
        for i in range(4):
            tb = 4 * ch + i
            c0 = (4 + i) * 16
            for half in range(2):
                V_(lambda e, tb=tb, c0=c0, half=half:
                   e.bn_stats(lnst[:, c0 + 4 + 6 * half: c0 + 10 + 6 * half],
                              xbuf[:, tb * D + half * 512: tb * D + (half + 1) * 512]),
                   reads=[f"x{tb}"], writes=[f"lnst{4 + i}"], tag=f"{tagp}bn{tb}_{half}")
            V_(lambda e, c0=c0: e.bn_aggr(lnst[:, c0:c0 + 2], lnst[:, c0 + 4:c0 + 16]),
               reads=[], writes=[f"lnst{4 + i}"], tag=f"{tagp}ag{tb}")
        wr = [f"lnst{4 + i}" for i in range(4)]
        S_(lambda e: e.activation(lnst[:, 66:115:16], lnst[:, 65:114:16], AF.Ln,
                                  bias=cstt[:, 0:1], scale=1.0),
           reads=["cstt"], writes=wr, tag=f"{tagp}ln4x{ch}")
        S_(lambda e: e.activation(lnst[:, 67:116:16], lnst[:, 66:115:16], AF.Exp,
                                  bias=0.0, scale=-0.5),
           reads=[], writes=wr, tag=f"{tagp}ex4x{ch}")
        V_(lambda e: e.scalar_tensor_tensor(lnst[:, 68:117:16], lnst[:, 64:113:16], -1.0,
                                            lnst[:, 67:116:16], ALU.mult, ALU.mult),
           reads=[], writes=wr, tag=f"{tagp}b4x{ch}")
        for i in range(4):
            c0 = (4 + i) * 16
            out[4 * ch + i] = (lnst[:, c0 + 3:c0 + 4], lnst[:, c0 + 4:c0 + 5],
                               f"lnst{4 + i}")
        return out

    ln3_sb = {}

    def ln3_tr(ch, i):
        tb = 4 * ch + i
        par = tb % 2
        lpar = 0
        rstd, bv, sres = ln3_sb[tb]
        V_(lambda e, tb=tb, par=par, rstd=rstd, bv=bv:
           e.tensor_scalar(lnx3[par][:], xbuf[:, tb * D:(tb + 1) * D], rstd, bv,
                           ALU.mult, ALU.add),
           reads=[f"x{tb}", sres], writes=["lnx3_0"], tag=f"ln3ap{tb}")
        ptres = f"pt{par}"
        for f in range(8):
            T_(lambda e, par=par, f=f:
               e.transpose(pt[:, par * 1024 + f * 128: par * 1024 + (f + 1) * 128],
                           lnx3[par][:, f * 128:(f + 1) * 128], eye[:]),
               reads=["lnx3_0", "eye"], writes=[ptres], tag=f"tr3_{tb}_{f}")
        V_(lambda e, par=par, i=i, lpar=lpar:
           e.tensor_copy(
               ln3T[lpar][:].rearrange("p (c t) -> p c t", c=8)[:, :, i * 128:(i + 1) * 128],
               pt[:, par * 1024:(par + 1) * 1024].rearrange("p (c t) -> p c t", c=8)),
           reads=[ptres], writes=[f"ln3T_{lpar}"], tag=f"evtr3_{tb}")

    def ln3_block(ch):
        ln3_sb.update(ln_stats4(ch, "ln3_"))
        for i in range(4):
            ln3_tr(ch, i)

    ln3_done = []

    ln1_sb = {}

    def ln1_tr(ch, i, dst, dres):
        tb = 4 * ch + i
        par = tb % 2
        rstd, bv, sres = ln1_sb[tb]
        V_(lambda e, tb=tb, par=par, rstd=rstd, bv=bv:
           e.tensor_scalar(lnx1[par][:], xbuf[:, tb * D:(tb + 1) * D], rstd, bv,
                           ALU.mult, ALU.add),
           reads=[f"x{tb}", sres], writes=[f"lnx1_{par}"], tag=f"ln1ap{tb}")
        for f in range(8):
            T_(lambda e, dst=dst, par=par, f=f:
               e.transpose(dst[:, f * 128:(f + 1) * 128],
                           lnx1[par][:, f * 128:(f + 1) * 128], eye[:]),
               reads=[f"lnx1_{par}", "eye"], writes=dres, tag=f"tr1_{tb}_{f}")
        S_(lambda e, dst=dst, i=i:
           e.copy(
               lnx1T[:].rearrange("p (c t) -> p c t", c=8)[:, :, i * 128:(i + 1) * 128],
               dst.rearrange("p (c t) -> p c t", c=8)),
           reads=dres, writes=["lnx1T"], tag=f"evtr1_{tb}")

    def ln1_block(ch):
        ln1_sb.update(ln_stats4(ch, "ln1_"))
        for i in range(4):
            par = (4 * ch + i) % 2
            ln1_tr(ch, i, pt[:, par * 1024:(par + 1) * 1024], [f"pt{par}"])

    ln1_block(0)
    for ch in range(n_attn):
        tbs = [4 * ch + i for i in range(4)]
        run_conv(9)

        for fop in (range(8) if attn_parts >= 2 else []):
            slot = fop % 2
            for c in range(8):
                for f2 in range(2):
                    fo = fop * 2 + f2
                    T_(lambda e, slot=slot, f2=f2, c=c, fo=fo:
                       e.matmul(pS[:, (slot * 2 + f2) * 512:(slot * 2 + f2 + 1) * 512],
                                wqkvb[:, c * 3072 + fo * 128: c * 3072 + (fo + 1) * 128],
                                lnx1T[:, c * 512:(c + 1) * 512],
                                start=(c == 0), stop=(c == 7)),
                       reads=["wqkvb", "lnx1T"], writes=[f"pS{slot * 2 + f2}"],
                       tag=f"qk{ch}_{fo}_{c}")
            for f2 in range(2):
                fo = fop * 2 + f2
                sl = slot * 2 + f2
                S_(lambda e, sl=sl, fo=fo:
                   e.add(qkT[:, fo * 512:(fo + 1) * 512], pS[:, sl * 512:(sl + 1) * 512],
                         bqtt[:, fo:fo + 1]),
                   reads=[f"pS{sl}", "bqtt"], writes=["qkT"], tag=f"qkev{ch}_{fo}")
        if ch + 1 < n_attn:
            ln1_sb.update(ln_stats4(ch + 1, "ln1_"))
        run_conv(9)

        def v_mm(i):
            par = i % 3
            for c in range(8):
                for half in range(2):
                    T_(lambda e, c=c, half=half, i=i:
                       e.matmul(p2[:, half * 512:(half + 1) * 512],
                                lnx1T[:, c * 512 + i * 128: c * 512 + (i + 1) * 128],
                                wqkvb[:, c * 3072 + 2048 + half * 512:
                                      c * 3072 + 2048 + (half + 1) * 512],
                                start=(c == 0), stop=(c == 7)),
                       reads=["lnx1T", "wqkvb"], writes=["p2"], tag=f"v{ch}_{i}_{c}_{half}")
            S_(lambda e, par=par: e.activation(vtok[par][:], p2[:], AF.Copy, bias=0.0,
                                               scale=1.0),
               reads=["p2"], writes=[f"vtok_{par}"], tag=f"vev{ch}_{i}")

        def s_mm(i):
            for h in [0, 2, 4, 6, 8, 10, 12, 14, 1, 3, 5, 7, 9, 11, 13, 15]:
                bp = 64 * (h % 2)
                fo = h // 2
                T_(lambda e, h=h, bp=bp, fo=fo, i=i:
                   e.matmul(pS[:, h * 128:(h + 1) * 128],
                            qkT[bp:bp + 64, fo * 512 + i * 128: fo * 512 + (i + 1) * 128],
                            qkT[bp:bp + 64, (8 + fo) * 512 + i * 128: (8 + fo) * 512 + (i + 1) * 128],
                            start=True, stop=True),
                   reads=["qkT"], writes=[f"pS{h // 4}"], tag=f"s{ch}_{i}_{h}")

        def softmax(i):
            for hh in range(2):
                V_(lambda e, hh=hh: e.scalar_tensor_tensor(
                       sbuf_sm[:, hh * 1024:(hh + 1) * 1024],
                       pS[:, hh * 1024:(hh + 1) * 1024], 0.125,
                       ballt[:, hh * 1024:(hh + 1) * 1024], ALU.mult, ALU.add),
                   reads=[f"pS{2 * hh}", f"pS{2 * hh + 1}", "ballt"],
                   writes=[f"sbpln{hh}"], tag=f"sb{ch}_{i}_{hh}")
            for hh in range(2):
                S_(lambda e, hh=hh: e.activation(
                       ubuf[:, hh * 1024:(hh + 1) * 1024],
                       sbuf_sm[:, hh * 1024:(hh + 1) * 1024],
                       AF.Exp, bias=cstt[:, 1:2], scale=1.0),
                   reads=[f"sbpln{hh}", "cstt"], writes=[f"ubuf{hh}"],
                   tag=f"exp{ch}_{i}_{hh}")
            for hh in range(2):
                V_(lambda e, hh=hh: e.reduce_sum(
                       lnst[:, 128 + 8 * hh:136 + 8 * hh],
                       ubuf[:, hh * 1024:(hh + 1) * 1024]
                       .rearrange("p (h j) -> p h j", h=8), axis=AX.X),
                   reads=[f"ubuf{hh}"], writes=[f"lnr{hh}"], tag=f"rsum{ch}_{i}_{hh}")
                V_(lambda e, hh=hh: e.reciprocal(lnst[:, 144 + 8 * hh:152 + 8 * hh],
                                                 lnst[:, 128 + 8 * hh:136 + 8 * hh]),
                   reads=[f"lnr{hh}"], writes=[f"lnrec{hh}"], tag=f"rec{ch}_{i}_{hh}")
                V_(lambda e, hh=hh: e.tensor_tensor(
                       ubuf[:, hh * 1024:(hh + 1) * 1024]
                       .rearrange("p (h j) -> p h j", h=8),
                       ubuf[:, hh * 1024:(hh + 1) * 1024]
                       .rearrange("p (h j) -> p h j", h=8),
                       lnst[:, 144 + 8 * hh:152 + 8 * hh].unsqueeze(2)
                       .broadcast_to([128, 8, 128]), ALU.mult),
                   reads=[f"lnrec{hh}"], writes=[f"ubuf{hh}"], tag=f"norm{ch}_{i}_{hh}")

        def ut_av(i):
            par = i % 3
            for hh in range(2):
                ptres = f"pt{hh}"
                for hl in range(8):
                    h = hh * 8 + hl
                    T_(lambda e, hh=hh, hl=hl, h=h:
                       e.transpose(pt[:, hh * 1024 + hl * 128: hh * 1024 + (hl + 1) * 128],
                                   ubuf[:, h * 128:(h + 1) * 128], eye[:]),
                       reads=[f"ubuf{hh}", "eye"], writes=[ptres], tag=f"trU{ch}_{i}_{h}")
                S_(lambda e, hh=hh:
                   e.copy(utb[:, hh * 1024:(hh + 1) * 1024], pt[:, hh * 1024:(hh + 1) * 1024]),
                   reads=[ptres], writes=[f"utb{hh}"], tag=f"utev{ch}_{i}_{hh}")
                for h in [hh * 8 + o for o in (0, 2, 4, 6, 1, 3, 5, 7)]:
                    T_(lambda e, h=h, par=par, hh=hh:
                       e.matmul(p2[64 * (h % 2):64 * (h % 2) + 64,
                                   (h // 2) * 128:(h // 2 + 1) * 128],
                                vtok[par][:, h * 64:(h + 1) * 64],
                                utb[:, h * 128:(h + 1) * 128], start=True, stop=True),
                       reads=[f"vtok_{par}", f"utb{hh}"], writes=["p2"],
                       tag=f"av{ch}_{i}_{h}")
            S_(lambda e, i=i:
               e.copy(
                   attnT[:].rearrange("p (g t) -> p g t", g=8)[:, :, i * 128:(i + 1) * 128],
                   p2[:].rearrange("p (g t) -> p g t", g=8)),
               reads=["p2"], writes=["attnT"], tag=f"avev{ch}_{i}")

        if attn_parts < 4:
            run_conv(9)
            continue
        if attn_parts < 8:
            for _i in range(4):
                v_mm(_i)
                if attn_parts >= 5:
                    s_mm(_i)
                if attn_parts >= 6:
                    softmax(_i)
                if attn_parts >= 7:
                    ut_av(_i)
            run_conv(9)
            continue
        v_mm(0)
        s_mm(0)
        softmax(0)
        v_mm(1)
        s_mm(1)
        ut_av(0)
        softmax(1)
        v_mm(2)
        s_mm(2)
        ut_av(1)
        softmax(2)
        v_mm(3)
        s_mm(3)
        ut_av(2)
        softmax(3)
        ut_av(3)
        run_conv(9)

        if ch == 0 and attn_parts >= 3:
            V_(lambda e: e.tensor_copy(beff[:], bprt[:]),
               reads=["bprt"], writes=["beff"], tag="beffcp")
            for c in range(8):
                cp = c // 2
                buf, bres = (wpA, "wpA") if cp % 2 == 0 else (wpB, "wpB")
                if c % 2 == 0:
                    DMA("S", f"wp{cp % 2}", buf[:],
                        bass.AP(wpscr.tensor, 2 * cp * 128 * 1024,
                                [[1024, 128], [128 * 1024, 2], [1, 1024]]),
                        deps=[markers["wp"]], writes=[bres], tag=f"wpld_beff{cp}")
                for half in range(2):
                    T_(lambda e, buf=buf, bres=bres, c=c, half=half:
                       e.matmul(p2[0:1, half * 512:(half + 1) * 512], bqvb[:, c:c + 1],
                                buf[:, (c % 2) * 1024 + half * 512:
                                    (c % 2) * 1024 + (half + 1) * 512],
                                start=(c == 0), stop=(c == 7)),
                       reads=[bres, "bqvb"], writes=["p2"], tag=f"beffmm{c}_{half}")
            S_(lambda e: e.copy(bvwp[:], p2[0:1, :]), reads=["p2"], writes=["bvwp"],
               tag="beffev")
            for half in range(2):
                T_(lambda e, half=half:
                   e.matmul(p2[:, half * 512:(half + 1) * 512], ones1[:],
                            bvwp[0:1, half * 512:(half + 1) * 512], start=True, stop=True),
                   reads=["bvwp", "ones1"], writes=["p2"], tag=f"beffbc{half}")
            V_(lambda e: e.tensor_add(beff[:], p2[:], beff[:]),
               reads=["p2", "beff"], writes=["beff"], tag="beffadd")


        ptf = pt[:].bitcast(F32)
        pjdsts = [pS[:, 0:1024], pS[:, 1024:2048], p2[:], ptf]
        pjres = [["pS0", "pS1"], ["pS2", "pS3"], ["p2"], ["pt0", "pt1"]]
        if attn_parts >= 9:
            for c in range(8):
                cp = c // 2
                buf, bres = (wpA, "wpA") if cp % 2 == 0 else (wpB, "wpB")
                if c % 2 == 0:
                    DMA("S", f"wp{cp % 2}", buf[:],
                        bass.AP(wpscr.tensor, 2 * cp * 128 * 1024,
                                [[1024, 128], [128 * 1024, 2], [1, 1024]]),
                        deps=[markers["wp"]], writes=[bres], tag=f"wpld{ch}_{cp}")
                for i in range(4):
                    for half in range(2):
                        T_(lambda e, buf=buf, bres=bres, c=c, i=i, half=half:
                           e.matmul(pjdsts[i][:, half * 512:(half + 1) * 512],
                                    attnT[:, c * 512 + i * 128: c * 512 + (i + 1) * 128],
                                    buf[:, (c % 2) * 1024 + half * 512:
                                        (c % 2) * 1024 + (half + 1) * 512],
                                    start=(c == 0), stop=(c == 7)),
                           reads=["attnT", bres],
                           writes=pjres[i], tag=f"pj{ch}_{i}_{c}_{half}")
            pSb = pS[:].bitcast(BF16)
            atf = attnT[:].bitcast(F32)
            pstage = [sbuf_sm[:, 0:1024], sbuf_sm[:, 1024:2048],
                      atf[:, 0:1024], atf[:, 1024:2048]]
            psres = ["sbpln0", "sbpln1", "attnT", "attnT"]
            for i in range(4):
                V_(lambda e, i=i:
                   e.scalar_tensor_tensor(pstage[i], pjdsts[i], 1.0,
                                          beff[:], ALU.mult, ALU.add),
                   reads=pjres[i] + ["beff"], writes=[psres[i]],
                   tag=f"pjev{ch}_{i}")
                if ch + 1 < n_attn:
                    ln1_tr(ch + 1, i, pSb[:, i * 1024:(i + 1) * 1024], [f"pS{i}"])
            if ch + 1 == n_attn and n_ffn:
                ln3_block(0)
                ln3_done.append(1)
            for i in range(4):
                tb = tbs[i]
                rstd, bv, sres = ln_chain(
                    lambda hf, i=i: pstage[i][:, hf * 512:(hf + 1) * 512], i,
                    psres[i], f"ln2_{tb}",
                    scratch=ubuf[:].bitcast(F32)[:, 512:1024], scres="ubuf")
                V_(lambda e, i=i, rstd=rstd, bv=bv:
                   e.tensor_scalar(pstage[i], pstage[i], rstd, bv, ALU.mult, ALU.add),
                   reads=[sres], writes=[psres[i]], tag=f"ln2ap{tb}")
                V_(lambda e, tb=tb, i=i:
                   e.tensor_add(xbuf[:, tb * D:(tb + 1) * D], xbuf[:, tb * D:(tb + 1) * D],
                                pstage[i]),
                   reads=[psres[i]], writes=[f"x{tb}"], tag=f"res2_{tb}")
        run_conv(9)

    if n_ffn and not ln3_done:
        ln3_block(0)
    for ch in range(n_ffn):
        tbs = [4 * ch + i for i in range(4)]
        lpar = 0

        def w0_load(j, ch=ch):
            buf, bres = (w0A, "w0A") if j % 2 == 0 else (w0B, "w0B")
            qeng = "S" if (ch == 0 and j < 4) else "Y"
            DMA(qeng, f"w0b{j % 2}", buf[:].rearrange("p (c j) -> p c j", c=8),
                bass.AP(w0scr.tensor, j * 512,
                        [[F0, 128], [128 * F0, 8], [1, 512]]),
                deps=[markers["w0"]], writes=[bres], tag=f"w0ld{ch}_{j}")

        for j in range(8):
            buf, bres = (w0A, "w0A") if j % 2 == 0 else (w0B, "w0B")
            if j >= 2 or ch == 0:
                w0_load(j)
            for m in range(4):
                f0b = 4 * j + m
                slot = f0b % 4
                for c in range(8):
                    T_(lambda e, buf=buf, bres=bres, m=m, c=c, slot=slot, lpar=lpar:
                       e.matmul(pS[:, slot * 512:(slot + 1) * 512],
                                buf[:, c * 512 + m * 128: c * 512 + m * 128 + 128],
                                ln3T[lpar][:, c * 512:(c + 1) * 512],
                                start=(c == 0), stop=(c == 7)),
                       reads=[bres, f"ln3T_{lpar}"], writes=[f"pS{slot}"],
                       tag=f"l0_{ch}_{f0b}_{c}")
                if m % 2 == 1:
                    fp = f0b // 2
                    pr = fp % 2
                    S_(lambda e, fp=fp, pr=pr:
                       e.activation(g0T[:, 2 * fp * 512:(2 * fp + 2) * 512],
                                    pS[:, pr * 1024:(pr + 1) * 1024], AF.Relu,
                                    bias=0.0, scale=sv2[:, 0:1]),
                       reads=[f"pS{pr * 2}", f"pS{pr * 2 + 1}", "sv2"], writes=["g0T"],
                       tag=f"g0ev{ch}_{fp}")
                    if fp == 0 and ch + 1 < n_ffn:
                        ln3_sb.update(ln_stats4(ch + 1, "ln3_"))

        def w2_load(j, ch=ch):
            buf, bres = (w2A, "w2A") if j % 2 == 0 else (w2B, "w2B")
            DMA("Y", f"w2b{j % 2}", buf[:].rearrange("p (b j) -> p b j", b=4),
                bass.AP(w2scr.tensor, 4 * j * 128 * 1024,
                        [[1024, 128], [128 * 1024, 4], [1, 1024]]),
                deps=[markers["w2"]], writes=[bres], tag=f"w2ld{ch}_{j}")

        w2_load(0)
        w2_load(1)
        if ch + 1 < n_ffn:
            for _i in range(4):
                ln3_tr(ch + 1, _i)
        for s in range(16):
            wb = s % 2
            DMA("Y", f"w1b{wb}",
                w1b[wb][:].rearrange("p (c j) -> p c j", c=32),
                bass.AP(w1scr.tensor, s * 256,
                        [[F1, 128], [128 * F1, 32], [1, 256]]),
                deps=[markers["w1"]],
                writes=[f"w1b_{wb}q{g}" for g in range(4)], tag=f"w1ld{ch}_{s}")
            if s == 1 and ch + 1 < n_ffn:
                w0_load(0, ch + 1)
                w0_load(1, ch + 1)
            base = 2 * (s % 2)
            for c in range(32):
                for fl in range(2):
                    T_(lambda e, wb=wb, c=c, fl=fl, base=base:
                       e.matmul(pS[:, (base + fl) * 512:(base + fl + 1) * 512],
                                w1b[wb][:, c * 256 + fl * 128: c * 256 + fl * 128 + 128],
                                g0T[:, c * 512:(c + 1) * 512],
                                start=(c == 0), stop=(c == 31)),
                       reads=[f"w1b_{wb}q{c // 8}", "g0T"], writes=[f"pS{base + fl}"],
                       tag=f"w1_{ch}_{s}_{c}_{fl}")
            S_(lambda e, s=s, base=base:
               e.activation(g1T[:, s * 1024:(s + 1) * 1024],
                            pS[:, (base // 2) * 1024:(base // 2 + 1) * 1024], AF.Relu,
                            bias=0.0, scale=sv2[:, 1:2]),
               reads=[f"pS{base}", f"pS{base + 1}", "sv2"], writes=["g1T"],
               tag=f"g1ev{ch}_{s}")


        ptf = pt[:].bitcast(F32)
        w2dsts = [pS[:, 0:1024], pS[:, 1024:2048], p2[:], ptf]
        w2res = [["pS0", "pS1"], ["pS2", "pS3"], ["p2"], ["pt0", "pt1"]]
        for fcp in range(16):
            j = fcp // 2
            buf, bres = (w2A, "w2A") if j % 2 == 0 else (w2B, "w2B")
            q = fcp % 2
            if fcp >= 4 and fcp % 2 == 0:
                w2_load(fcp // 2)
            for fl in range(2):
                fc = 2 * fcp + fl
                for i in range(4):
                    for half in range(2):
                        T_(lambda e, buf=buf, bres=bres, q=q, fl=fl, fc=fc, half=half, i=i:
                           e.matmul(w2dsts[i][:, half * 512:(half + 1) * 512],
                                    g1T[:, fc * 512 + i * 128: fc * 512 + (i + 1) * 128],
                                    buf[:, q * 2048 + fl * 1024 + half * 512:
                                        q * 2048 + fl * 1024 + (half + 1) * 512],
                                    start=(fc == 0), stop=(fc == 31)),
                           reads=[bres, "g1T"], writes=w2res[i],
                           tag=f"w2_{ch}_{i}_{fc}_{half}")
        hbufs = [(hbuf[:], "hbuf"), (w2A[:].bitcast(F32)[:, 0:1024], "w2A")]

        def h2ev(i, ch=ch):
            hb, hres = hbufs[i % 2]
            V_(lambda e, i=i, hb=hb:
               e.tensor_scalar(hb, w2dsts[i], sv2[:, 2:3], None, ALU.mult),
               reads=w2res[i] + ["sv2"], writes=[hres], tag=f"h2ev{ch}_{i}")

        def ln4_fin(i):
            tb = tbs[i]
            hb, hres = hbufs[i % 2]
            rstd, bv, sres = ln_chain(
                lambda half, hb=hb: hb[:, half * 512:(half + 1) * 512], i, hres,
                f"ln4_{tb}",
                scratch=lnx3[0][:].bitcast(F32)[:, 0:512], scres="lnx3_0")
            V_(lambda e, hb=hb, rstd=rstd, bv=bv:
               e.tensor_scalar(hb, hb, rstd, bv, ALU.mult, ALU.add),
               reads=[sres], writes=[hres], tag=f"ln4ap{tb}")
            V_(lambda e, tb=tb, hb=hb:
               e.tensor_add(xbuf[:, tb * D:(tb + 1) * D], xbuf[:, tb * D:(tb + 1) * D],
                            hb),
               reads=[hres], writes=[f"x{tb}"], tag=f"res4_{tb}")
            DMA("S", "outq", out[tb * 128:(tb + 1) * 128, :], xbuf[:, tb * D:(tb + 1) * D],
                reads=[f"x{tb}"], tag=f"st{tb}")

        h2ev(0)
        h2ev(1)
        ln4_fin(0)
        h2ev(2)
        ln4_fin(1)
        h2ev(3)
        ln4_fin(2)
        ln4_fin(3)

    assert (not do_conv) or n_attn < NCH or conv_pos[0] == len(conv_jobs), \
        f"conv jobs left: {len(conv_jobs) - conv_pos[0]}"
    if do_conv and n_attn == NCH:
        assert set(markers) == {"wp", "w0", "w1", "w2"}, markers.keys()

    P.finalize(sem_handles)

    def make_body(ename):
        stream = P.streams[ename]
        csem = sem_handles["c" + ename]

        def body(eng):
            for op in stream:
                for (sem, val) in op.waits:
                    eng.wait_ge(sem, val)
                if op.emit is None:
                    continue
                inst = op.emit(eng)
                if op.is_dma:
                    inst.then_inc(sem_handles[op.semkey], 16)
                elif op.need_inc:
                    inst.then_inc(csem, 1)
        return body

    fin = Op("Y", None, False, "fin")
    fin.waits = []
    for k, v in P.totals["d"].items():
        fin.waits.append((sem_handles[k], v))
    for e in ENGINES:
        if P.totals["c"][e]:
            fin.waits.append((sem_handles["c" + e], P.totals["c"][e]))
    P.streams["Y"].append(fin)

    with nc.Block() as block:
        block.tensor(make_body("T"))
        block.vector(make_body("V"))
        block.scalar(make_body("S"))
        block.gpsimd(make_body("G"))
        block.sync(make_body("Y"))
    st.close()
    return nc


_NC_CACHE = []


def kernel(L0_attn_x, L0_attn_w_qkv, L0_attn_b_qkv, L0_attn_w_proj, L0_attn_b_proj,
           L0_attn_bias_table, L0_ffn_k0, L0_ffn_k1, L0_ffn_k2, L0_ffn_s0, L0_ffn_s1,
           L0_ffn_s2):
    x = np.ascontiguousarray(np.asarray(L0_attn_x, np.float32))
    wqkv = np.ascontiguousarray(np.asarray(L0_attn_w_qkv, np.float32))
    wproj = np.ascontiguousarray(np.asarray(L0_attn_w_proj, np.float32))
    bqkv = np.asarray(L0_attn_b_qkv, np.float32)
    bproj = np.asarray(L0_attn_b_proj, np.float32)
    btab = np.asarray(L0_attn_bias_table, np.float32)
    k0 = np.ascontiguousarray(np.asarray(L0_ffn_k0).astype(np.uint8))
    k1 = np.ascontiguousarray(np.asarray(L0_ffn_k1).astype(np.uint8))
    k2 = np.ascontiguousarray(np.asarray(L0_ffn_k2).astype(np.uint8))
    s0 = np.float32(L0_ffn_s0)
    s1 = np.float32(L0_ffn_s1)
    s2 = np.float32(L0_ffn_s2)

    idx = _rel_index()
    bias = btab[idx.reshape(-1), :].reshape(WS, WS, H)
    ball = np.full((128, 2048), NEG, np.float32)
    for h in range(H):
        for a in range(16):
            ball[a * 8:(a + 1) * 8, h * 128 + a * 8: h * 128 + (a + 1) * 8] = bias[:, :, h]

    bqt = np.ascontiguousarray(bqkv.reshape(24, 128).T)
    bpr = np.ascontiguousarray(np.broadcast_to(bproj[None, :], (128, 1024)))
    sv = np.ascontiguousarray(np.broadcast_to(
        np.stack([s0, s1, s2])[None, :], (128, 3))).astype(np.float32)
    eyeb = np.eye(128, dtype=np.float32).astype(ml_dtypes.bfloat16)
    oneb = np.ones((1, 128), ml_dtypes.bfloat16)
    cst = np.ascontiguousarray(np.broadcast_to(
        np.array([EPS, -20.0], np.float32)[None, :], (128, 2)))

    rows = []
    in_maps = []
    for core in range(NCORES):
        b, r = divmod(core, 4)
        rr = (np.arange(2048 * r, 2048 * (r + 1)) + SHIFT) % SEQ
        rows.append((b, rr))
        xs = np.ascontiguousarray(x[b, rr, :])
        in_maps.append(dict(xs=xs, wqkv=wqkv, wproj=wproj, k0=k0, k1=k1, k2=k2,
                            ball=ball, bqt=bqt, bpr=bpr, svin=sv, eyein=eyeb,
                            onein=oneb, cstin=cst))

    if not _NC_CACHE:
        _NC_CACHE.append(build_program())
    nc = _NC_CACHE[0]
    res = run_bass_kernel_spmd(nc, in_maps, core_ids=list(range(NCORES)))

    outp = np.empty((2, SEQ, D), np.float32)
    for core in range(NCORES):
        b, rr = rows[core]
        outp[b, rr, :] = np.asarray(res.results[core]["out"], np.float32)
    return outp
